# version 02
# baseline (speedup 1.0000x reference)
import numpy as np
import ml_dtypes
import concourse.bass as bass
import concourse.mybir as mybir
from concourse.bass_utils import run_bass_kernel_spmd
from contextlib import ExitStack

F32 = mybir.dt.float32
BF16 = mybir.dt.bfloat16
AF = mybir.ActivationFunctionType
ALU = mybir.AluOpType
AX = mybir.AxisListType

B, S, D = 2, 8192, 1024
H = 16
FF, NE = 4096, 8
TOWN, NCH = 2048, 16
NG, GSZ = 16, 4
NPAIR = 8
SC = 1.0 / 32.0
EPS_LN, EPS_RMS = 1e-5, 1e-6


def _pieces(g):
    qlo = 128 * g
    out = []
    p = qlo
    fe = min(2048, (qlo // 512 + 1) * 512)
    if fe > p:
        out.append((p, fe))
        p = fe
    while p < 2048:
        out.append((p, p + 512))
        p += 512
    return out


def _lspans(g):
    qlo = 128 * g
    out = []
    p = qlo
    fe = min(TOWN, (qlo // 256 + 1) * 256)
    if fe > p:
        out.append((p, fe))
        p = fe
    while p < TOWN:
        out.append((p, p + 256))
        p += 256
    return out


class Sem:
    def __init__(self, nc, name):
        self.h = nc.alloc_semaphore(name)
        self.n = 0

    def inc(self, instr, k=1):
        instr.then_inc(self.h, k)
        self.n += k
        return self.n


def _build(dbg, upto, use, rec, sub=""):
    nc = bass.Bass("TRN2", target_bir_lowering=False)
    DT = nc.dram_tensor
    x_own = DT("x_own", [TOWN, D], F32, kind="ExternalInput").ap()
    x_str = DT("x_str", [S, D], F32, kind="ExternalInput").ap()
    wqkv = DT("wqkv", [D, 3 * D], F32, kind="ExternalInput").ap()
    w_o_in = DT("w_o_in", [D, D], F32, kind="ExternalInput").ap()
    rms_bc = DT("rms_bc", [128, D], F32, kind="ExternalInput").ap()
    gate_t = DT("gate_t", [D, NE], F32, kind="ExternalInput").ap()
    b1_t = DT("b1_t", [128, NE * 32], F32, kind="ExternalInput").ap()
    b2_in = DT("b2_in", [NE, D], F32, kind="ExternalInput").ap()
    w1_2d = DT("w1_2d", [NE * D, FF], F32, kind="ExternalInput").ap()
    w2_2d = DT("w2_2d", [NE * FF, D], F32, kind="ExternalInput").ap()
    masks = DT("masks", [4, 128, 128], F32, kind="ExternalInput").ap()
    ident = DT("ident", [128, 128], F32, kind="ExternalInput").ap()
    out_t = DT("out", [TOWN, D], F32, kind="ExternalOutput").ap()
    dkind = "ExternalOutput" if dbg else "Internal"
    xn2t_sp = DT("xn2t_sp", [128, 8 * TOWN], BF16, kind=dkind).ap()
    attnx_sp = DT("attnx_sp", [TOWN, D], F32, kind=dkind).ap()
    l_tab = DT("l_tab", [H, NG, TOWN], F32, kind=dkind).ap()
    l_tab2 = l_tab.rearrange("h g q -> (h g) q")
    if dbg:
        d_qt = DT("d_qt", [128, NPAIR * TOWN], BF16, kind="ExternalOutput").ap()
        d_coef = DT("d_coef", [128, NCH * NE], F32, kind="ExternalOutput").ap()
        d_ot = DT("d_ot", [128, 8 * TOWN], F32, kind="ExternalOutput").ap()

    coeft_sp = DT("coeft_sp", [8, NCH * 128], F32, kind="Internal").ap()
    coef_sp = DT("coef_sp", [128, NCH * NE], F32, kind="Internal").ap()
    ctx = ExitStack()
    E = ctx.enter_context
    maskb = E(nc.sbuf_tensor("maskb", [128, 4 * 128], BF16))
    id_b = E(nc.sbuf_tensor("id_b", [128, 128], BF16))
    eps5 = E(nc.sbuf_tensor("eps5", [128, 1], F32))
    ctxB = ExitStack()
    EB = ctxB.enter_context
    wq_kv = EB(nc.sbuf_tensor("wq_kv", [128, 8 * 2048], BF16, side="right"))
    QT = [EB(nc.sbuf_tensor(f"qt{g}", [128, TOWN], BF16, side="right")) for g in range(NPAIR)]

    soV = Sem(nc, "soV")
    soA = Sem(nc, "soA")
    soT = Sem(nc, "soT")
    sw = Sem(nc, "s_w")
    szf = Sem(nc, "s_zf")
    ssp = Sem(nc, "s_sp")
    sldma = Sem(nc, "s_ldma")
    smc = Sem(nc, "s_mc")
    sout_ = Sem(nc, "s_out")
    sxl = Sem(nc, "s_xl")
    sxsb = [Sem(nc, f"s_xs{i}") for i in range(4)]
    slt = [Sem(nc, f"s_lt{i}") for i in range(2)]
    sxo = [Sem(nc, f"s_xo{i}") for i in range(2)]
    sax = [Sem(nc, f"s_ax{i}") for i in range(2)]
    sw1 = [Sem(nc, f"s_w1{i}") for i in range(2)]
    sw2 = [Sem(nc, f"s_w2{i}") for i in range(2)]
    sat = [Sem(nc, f"s_at{i}") for i in range(2)]
    sot = [Sem(nc, f"s_ot{i}") for i in range(2)]

    def cv(v, name, *a, **kw):
        v.wait_ge(soV.h, soV.n)
        ins = getattr(v, name)(*a, **kw)
        soV.inc(ins)
        return ins

    def cs(s, name, *a, **kw):
        s.wait_ge(soA.h, soA.n)
        ins = getattr(s, name)(*a, **kw)
        soA.inc(ins)
        return ins

    def mark(key, sem):
        rec[key] = sem.n

    def wg(eng, sem, key):
        eng.wait_ge(sem.h, use.get(key, 0))

    with ExitStack() as sctx:
        zb = sctx.enter_context(nc.sbuf_tensor("zb", [64, 8192], F32))
        with nc.Block("setup") as blk:
            @blk.gpsimd
            def _(g):
                sw.inc(g.dma_start(wq_kv[:].rearrange("p (c j) -> p c j", c=8),
                                   wqkv[:, D:3 * D].rearrange("(c p) j -> p c j", p=128)), 16)
                sw.inc(g.dma_start(maskb[:].rearrange("p (r q) -> p r q", r=4),
                                   masks.rearrange("r k q -> k r q")), 16)
                sw.inc(g.dma_start(id_b[:], ident), 16)

            @blk.vector
            def _(v):
                cv(v, "memset", zb[:], 0.0)
                cv(v, "memset", eps5[:], EPS_LN)
                mark("zb", soV)

            @blk.sync
            def _(sy):
                wg(sy, soV, "zb")
                lt_flat = l_tab.rearrange("h g q -> (h g q)").rearrange("(a b) -> a b", a=64)
                szf.inc(sy.dma_start(lt_flat[:, :], zb[:]), 16)
                sy.wait_ge(szf.h, 16)

    with ExitStack() as actx:
        Ea = actx.enter_context
        wq_q = Ea(nc.sbuf_tensor("wq_q", [128, 8 * D], BF16))
        rmsb = Ea(nc.sbuf_tensor("rmsb", [128, D], F32))
        coefT = Ea(nc.sbuf_tensor("coefT", [8, NCH * 128], F32))
        id_f = Ea(nc.sbuf_tensor("id_f", [128, 128], F32))
        coef = Ea(nc.sbuf_tensor("coef", [128, NCH * NE], F32))
        eps6 = Ea(nc.sbuf_tensor("eps6", [128, 1], F32))
        ghi = Ea(nc.sbuf_tensor("ghi", [128, 8 * NE], BF16))
        glo = Ea(nc.sbuf_tensor("glo", [128, 8 * NE], BF16))
        gf = Ea(nc.sbuf_tensor("gf", [128, 8 * NE], F32))
        xbuf = [Ea(nc.sbuf_tensor(f"a0x{i}", [128, D], F32)) for i in range(4)]
        xnb = [Ea(nc.sbuf_tensor(f"a0n{i}", [128, D], BF16)) for i in range(2)]
        xn2b = [Ea(nc.sbuf_tensor(f"a0n2{i}", [128, D], BF16)) for i in range(2)]
        lob = [Ea(nc.sbuf_tensor(f"a0lo{i}", [128, D], BF16)) for i in range(2)]
        xnf = Ea(nc.sbuf_tensor("a0xnf", [128, D], F32))
        xrf = Ea(nc.sbuf_tensor("a0xrf", [128, D], F32))
        x2f = Ea(nc.sbuf_tensor("a0x2f", [128, D], F32))
        bst = Ea(nc.sbuf_tensor("a0bst", [128, 48], F32))
        mv = Ea(nc.sbuf_tensor("a0mv", [128, 8], F32))
        xnt = Ea(nc.sbuf_tensor("a0xnt", [128, 8 * 512], BF16))
        x2t = Ea(nc.sbuf_tensor("a0x2t", [128, 8 * 512], BF16))
        lot = [Ea(nc.sbuf_tensor(f"a0lot{i}", [128, 8 * 128], BF16)) for i in range(2)]
        cst = {nm: Ea(nc.sbuf_tensor("a0" + nm, [128, 4], F32))
               for nm in "vv rec ms lnv rstd lnm cc".split()}
        g16 = {nm: Ea(nc.sbuf_tensor("a0g" + nm, [128, 16], F32))
               for nm in "gm ngm gs gr".split()}
        Lsb = Ea(nc.sbuf_tensor("a0L", [128, NCH * NE], F32))
        egs = Ea(nc.sbuf_tensor("a0eg", [128, NE], F32))
        ps_tr = [Ea(nc.psum_tensor(f"a0ptr{i}", [128, 1024], BF16)) for i in range(2)]
        ps_q = [Ea(nc.psum_tensor(f"a0pq{i}", [128, 512], F32)) for i in range(2)]
        ps_g = [Ea(nc.psum_tensor(f"a0pg{i}", [128, NE], F32)) for i in range(2)]
        ps_ct = Ea(nc.psum_tensor("a0pct", [8, 128], F32))

        with nc.Block("a0") as blk:
            @blk.gpsimd
            def _(g):
                sw.inc(g.dma_start(wq_q[:].rearrange("p (c j) -> p c j", c=8),
                                   wqkv[:, 0:D].rearrange("(c p) j -> p c j", p=128)), 16)

            @blk.sync
            def _(sy):
                sxl.inc(sy.dma_start(rmsb[:], rms_bc), 16)
                sxl.inc(sy.dma_start(id_f[:], ident), 16)
                sxl.inc(sy.dma_start(gf[:].rearrange("p (c e) -> p c e", c=8),
                                     gate_t.rearrange("(c p) e -> p c e", p=128)), 16)

                def spills(st):
                    if "nospill" in sub:
                        return
                    wg(sy, soA, ("a0te_ch", st * 4 + 3))
                    for cd in range(8):
                        ssp.inc(sy.dma_start(xn2t_sp[:, cd * TOWN + st * 512:cd * TOWN + (st + 1) * 512],
                                             x2t[:, cd * 512:(cd + 1) * 512]), 16)
                for st in range(4):
                    for i in range(4):
                        ch = st * 4 + i
                        if ch >= 4:
                            wg(sy, soV, ("a0ln", ch - 4))
                        sxsb[ch % 4].inc(sy.dma_start(xbuf[ch % 4][:], x_own[ch * 128:(ch + 1) * 128, :]), 16)
                    if st >= 1:
                        spills(st - 1)
                spills(3)

            @blk.vector
            def _(v):
                c = cst
                cv(v, "memset", eps6[:], EPS_RMS)
                v.wait_ge(sxl.h, 48)
                cv(v, "tensor_copy", ghi[:], gf[:])
                cv(v, "scalar_tensor_tensor", glo[:], ghi[:], -1.0, gf[:], ALU.mult, ALU.add)
                mark("gate_hl", soV)
                for st in range(4):
                    for i in range(4):
                        ch = st * 4 + i
                        v.wait_ge(sxsb[ch % 4].h, 16 * (ch // 4 + 1))
                        cv(v, "bn_stats", bst[:, i * 12:i * 12 + 6], xbuf[ch % 4][:, 0:512])
                        cv(v, "bn_stats", bst[:, i * 12 + 6:i * 12 + 12], xbuf[ch % 4][:, 512:1024])
                    for i in range(4):
                        cv(v, "bn_aggr", mv[:, i * 2:i * 2 + 2], bst[:, i * 12:i * 12 + 12])
                    for i in range(4):
                        cv(v, "tensor_scalar", c["vv"][:, i:i + 1], mv[:, 2 * i + 1:2 * i + 2],
                           EPS_LN, None, ALU.add)
                    for i in range(4):
                        cv(v, "reciprocal", c["rec"][:, i:i + 1], c["vv"][:, i:i + 1])
                    for i in range(4):
                        cv(v, "tensor_mul", c["ms"][:, i:i + 1], mv[:, 2 * i + 1:2 * i + 2],
                           c["rec"][:, i:i + 1])
                    mark(("a0var", st), soV)
                    wg(v, soA, ("a0rstd", st))
                    for i in range(4):
                        ch = st * 4 + i
                        if ch >= 2:
                            wg(v, soT, ("a0tr", ch - 2))
                        cv(v, "tensor_scalar", xnf[:], xbuf[ch % 4][:], mv[:, 2 * i:2 * i + 1],
                           c["rstd"][:, i:i + 1], ALU.subtract, ALU.mult)
                        cv(v, "tensor_copy", xnb[ch % 2][:], xnf[:])
                        cv(v, "tensor_mul", xrf[:], xnf[:], rmsb[:])
                        cv(v, "tensor_scalar", x2f[:], xrf[:], c["cc"][:, i:i + 1], None, ALU.mult)
                        cv(v, "tensor_copy", xn2b[ch % 2][:], x2f[:])
                        cv(v, "scalar_tensor_tensor", lob[ch % 2][:], xn2b[ch % 2][:], -1.0, x2f[:],
                           ALU.mult, ALU.add)
                        mark(("a0ln", ch), soV)

            @blk.tensor
            def _(t):
                if "notr" in sub:
                    return
                t.wait_ge(sw.h, 64)
                wg(t, soV, "gate_hl")
                for st in range(4):
                    for i in range(4):
                        ch = st * 4 + i
                        wg(t, soV, ("a0ln", ch))
                        for sj, src in enumerate((xnb[ch % 2], xn2b[ch % 2], lob[ch % 2])):
                            for j in range(8):
                                ntr = 24 * ch + 8 * sj + j
                                if ntr >= 2:
                                    wg(t, soA, ("a0te", ntr - 2))
                                soT.inc(t.transpose(ps_tr[ntr % 2][:, 0:128],
                                                    src[:, j * 128:(j + 1) * 128], id_b[:]))
                                mark(("a0trj", ntr), soT)
                        mark(("a0tr", ch), soT)
                        if "nogate" in sub:
                            continue
                        wg(t, soA, ("a0te_ch", ch))
                        if ch >= 2:
                            wg(t, soA, ("a0gl", ch - 2))
                        nmm = 0
                        ins = None
                        for cd in range(8):
                            for src2 in range(3):
                                if src2 == 0:
                                    lhsT, rh = x2t[:, cd * 512 + i * 128:cd * 512 + i * 128 + 128], ghi
                                elif src2 == 1:
                                    lhsT, rh = x2t[:, cd * 512 + i * 128:cd * 512 + i * 128 + 128], glo
                                else:
                                    lhsT, rh = lot[ch % 2][:, cd * 128:(cd + 1) * 128], ghi
                                nmm += 1
                                ins = t.matmul(ps_g[ch % 2][:], lhsT, rh[:, cd * NE:(cd + 1) * NE],
                                               start=(nmm == 1), stop=(nmm == 24))
                        soT.inc(ins)
                        mark(("a0gate", ch), soT)
                    for gp in range(8):
                        if "noqt" in sub:
                            break
                        k = st * 8 + gp
                        if k >= 2:
                            wg(t, soA, ("a0qe", k - 2))
                        ins = None
                        for cd in range(8):
                            ins = t.matmul(ps_q[k % 2][:], wq_q[:, cd * D + gp * 128:cd * D + (gp + 1) * 128],
                                           xnt[:, cd * 512:(cd + 1) * 512], start=(cd == 0), stop=(cd == 7))
                        soT.inc(ins)
                        mark(("a0qt", k), soT)
                if "nogate" not in sub:
                    wg(t, soV, ("a0coef", NCH - 1))
                    for ch in range(NCH):
                        if ch >= 1:
                            wg(t, soV, ("a0ctv", ch - 1))
                        soT.inc(t.transpose(ps_ct[:], coef[:, ch * NE:(ch + 1) * NE], id_f[:]))
                        mark(("a0ct", ch), soT)

            @blk.scalar
            def _(s):
                c = cst
                for st in range(4):
                    wg(s, soV, ("a0var", st))
                    for i in range(4):
                        cs(s, "activation", c["lnv"][:, i:i + 1], mv[:, 2 * i + 1:2 * i + 2], AF.Ln, bias=eps5[:])
                    for i in range(4):
                        cs(s, "activation", c["lnm"][:, i:i + 1], c["ms"][:, i:i + 1], AF.Ln, bias=eps6[:])
                    for i in range(4):
                        cs(s, "activation", c["rstd"][:, i:i + 1], c["lnv"][:, i:i + 1], AF.Exp, scale=-0.5)
                    for i in range(4):
                        cs(s, "activation", c["cc"][:, i:i + 1], c["lnm"][:, i:i + 1], AF.Exp, scale=-0.5)
                    mark(("a0rstd", st), soA)
                    if "notr" in sub or "noev" in sub:
                        continue
                    for i in range(4):
                        ch = st * 4 + i
                        if ch >= 2:
                            wg(s, soT, ("a0gate", ch - 2))
                        if i == 0 and st >= 1:
                            wg(s, soT, ("a0qt", st * 8 - 1))
                            s.wait_ge(ssp.h, 128 * st)
                        for sj, dst in enumerate(("xnt", "x2t", "lot")):
                            for j in range(8):
                                ntr = 24 * ch + 8 * sj + j
                                wg(s, soT, ("a0trj", ntr))
                                psl = ps_tr[ntr % 2][:, 0:128]
                                if dst == "xnt":
                                    cs(s, "activation", xnt[:, j * 512 + i * 128:j * 512 + i * 128 + 128],
                                       psl, AF.Copy)
                                elif dst == "x2t":
                                    cs(s, "activation", x2t[:, j * 512 + i * 128:j * 512 + i * 128 + 128],
                                       psl, AF.Copy)
                                else:
                                    cs(s, "activation", lot[ch % 2][:, j * 128:(j + 1) * 128], psl, AF.Copy)
                                mark(("a0te", ntr), soA)
                        mark(("a0te_ch", ch), soA)
                        if "nogate" not in sub:
                            wg(s, soT, ("a0gate", ch))
                            cs(s, "activation", Lsb[:, ch * NE:(ch + 1) * NE], ps_g[ch % 2][:], AF.Copy)
                            mark(("a0gl", ch), soA)
                    for gp in range(8):
                        if "noqt" in sub:
                            break
                        k = st * 8 + gp
                        wg(s, soT, ("a0qt", k))
                        cs(s, "activation", QT[gp][:, st * 512:(st + 1) * 512], ps_q[k % 2][:], AF.Copy)
                        mark(("a0qe", k), soA)
                if "notr" not in sub and "nogate" not in sub:
                    for ch in range(NCH):
                        wg(s, soV, ("a0ngm", ch))
                        cs(s, "activation", egs[:], Lsb[:, ch * NE:(ch + 1) * NE], AF.Exp,
                           bias=g16["ngm"][:, ch:ch + 1], accum_out=g16["gs"][:, ch:ch + 1])
                        mark(("a0ga", ch), soA)

            @blk.vector
            def _(v):
                if "notr" in sub or "nogate" in sub:
                    return
                for ch in range(NCH):
                    wg(v, soA, ("a0gl", ch))
                    cv(v, "reduce_max", g16["gm"][:, ch:ch + 1], Lsb[:, ch * NE:(ch + 1) * NE], axis=AX.X)
                for ch in range(NCH):
                    cv(v, "tensor_scalar", g16["ngm"][:, ch:ch + 1], g16["gm"][:, ch:ch + 1],
                       -1.0, None, ALU.mult)
                    mark(("a0ngm", ch), soV)
                for ch in range(NCH):
                    wg(v, soA, ("a0ga", ch))
                    cv(v, "reciprocal", g16["gr"][:, ch:ch + 1], g16["gs"][:, ch:ch + 1])
                for ch in range(NCH):
                    cv(v, "tensor_scalar", coef[:, ch * NE:(ch + 1) * NE],
                       Lsb[:, ch * NE:(ch + 1) * NE],
                       g16["gm"][:, ch:ch + 1], g16["gr"][:, ch:ch + 1], ALU.is_equal, ALU.mult)
                    mark(("a0coef", ch), soV)
                for ch in range(NCH):
                    wg(v, soT, ("a0ct", ch))
                    cv(v, "tensor_copy", coefT[:, ch * 128:(ch + 1) * 128], ps_ct[:])
                    mark(("a0ctv", ch), soV)

            @blk.sync
            def _(sy):
                if "nogate" not in sub:
                    wg(sy, soV, ("a0ctv", NCH - 1))
                    ssp.inc(sy.dma_start(coeft_sp, coefT[:]), 16)
                    wg(sy, soV, ("a0coef", NCH - 1))
                    ssp.inc(sy.dma_start(coef_sp, coef[:]), 16)
                if dbg and "notr" not in sub:
                    wg(sy, soA, ("a0qe", 31))
                    for gp in range(NPAIR):
                        ssp.inc(sy.dma_start(d_qt[:, gp * TOWN:(gp + 1) * TOWN], QT[gp][:]), 16)
                    wg(sy, soV, ("a0coef", NCH - 1))
                    ssp.inc(sy.dma_start(d_coef, coef[:]), 16)
                sy.wait_ge(ssp.h, ssp.n)

    if upto == "a0":
        with nc.Block("fin0") as blk:
            @blk.sync
            def _(sy):
                sout_.inc(sy.dma_start(out_t, x_own), 16)
                sy.wait_ge(sout_.h, 16)
        ctx.close()
        return nc

    scg = []
    tot = 0
    for g in range(NG):
        tot += H * 4 * len(_pieces(g))
        scg.append(tot)

    with ExitStack() as wctx:
        OTacc = wctx.enter_context(nc.sbuf_tensor("OTacc", [128, 8 * TOWN], F32))
        with ExitStack() as a1c:
            E1 = a1c.enter_context
            xs = [E1(nc.sbuf_tensor(f"a1x{i}", [128, D], F32)) for i in range(4)]
            xnb1 = [E1(nc.sbuf_tensor(f"a1n{i}", [128, D], BF16)) for i in range(2)]
            bst1 = E1(nc.sbuf_tensor("a1bst", [128, 48], F32))
            mv1 = E1(nc.sbuf_tensor("a1mv", [128, 8], F32))
            xnt1 = E1(nc.sbuf_tensor("a1xnt", [128, 8 * 512], BF16))
            ktg = E1(nc.sbuf_tensor("a1kt", [128, 8 * 512], BF16))
            vg = [E1(nc.sbuf_tensor(f"a1v{i}", [128, H * 65], BF16)) for i in range(4)]
            PT = [[E1(nc.sbuf_tensor(f"a1pt{s0}_{k}", [128, TOWN], BF16)) for k in range(4)]
                  for s0 in range(2)]
            tmpl = E1(nc.sbuf_tensor("a1tl", [1, 256], F32))
            c1 = {nm: E1(nc.sbuf_tensor("a1" + nm, [128, 4], F32))
                  for nm in "lnv rstd".split()}
            ps_s = E1(nc.psum_tensor("a1ps", [128, 1024], F32))
            ps_big = E1(nc.psum_tensor("a1pb", [128, 2048], F32))
            ps_tr1 = [E1(nc.psum_tensor(f"a1ptr{i}", [128, 1024], BF16)) for i in range(2)]

            nld_tab = {}
            nld = 0
            for g in range(NG):
                for h in range(H):
                    for li in range(len(_lspans(g))):
                        nld_tab[(g, h, li)] = nld
                        nld += 1
            TOT_L = nld

            with nc.Block("a1") as blk:
                @blk.sync
                def _(sy):
                    for s_ in range(64):
                        if s_ >= 4:
                            wg(sy, soV, ("a1xnb", s_ - 4))
                        sxsb[s_ % 4].inc(sy.dma_start(xs[s_ % 4][:], x_str[s_ * 128:(s_ + 1) * 128, :]), 16)

                @blk.gpsimd
                def _(g_):
                    for g in range(NG):
                        for h in range(H):
                            for li, (a, b) in enumerate(_lspans(g)):
                                wg(g_, soV, ("a1lev", g, h, li))
                                sldma.inc(g_.dma_start(
                                    l_tab2[h * NG + g:h * NG + g + 1, a:b],
                                    tmpl[0:1, 0:b - a],
                                    accum_op=(ALU.add if g > 0 else ALU.bypass)), 16)
                    g_.wait_ge(sldma.h, sldma.n)

                @blk.vector
                def _(v):
                    for i in range(4):
                        cv(v, "memset", vg[i][:].rearrange("p (h j) -> p h j", j=65)[:, :, 64:65], 1.0)

                    def stats(g):
                        for i in range(4):
                            s_ = 4 * g + i
                            v.wait_ge(sxsb[s_ % 4].h, 16 * (4 + s_ // 4 + 1))
                            cv(v, "bn_stats", bst1[:, i * 12:i * 12 + 6], xs[s_ % 4][:, 0:512])
                            cv(v, "bn_stats", bst1[:, i * 12 + 6:i * 12 + 12], xs[s_ % 4][:, 512:1024])
                        for i in range(4):
                            cv(v, "bn_aggr", mv1[:, i * 2:i * 2 + 2], bst1[:, i * 12:i * 12 + 12])
                        mark(("a1var", g), soV)

                    def xnbs(g):
                        wg(v, soA, ("a1rstd", g))
                        for i in range(4):
                            s_ = 4 * g + i
                            if s_ >= 2:
                                wg(v, soT, ("a1tr", (s_ - 2) // 4, (s_ - 2) % 4))
                            cv(v, "tensor_scalar", xnb1[s_ % 2][:], xs[s_ % 4][:], mv1[:, 2 * i:2 * i + 1],
                               c1["rstd"][:, i:i + 1], ALU.subtract, ALU.mult)
                            mark(("a1xnb", s_), soV)

                    stats(0)
                    xnbs(0)
                    for g in range(NG):
                        for i in range(4):
                            wg(v, soT, ("a1vm", g, i))
                            if g >= 1:
                                wg(v, soT, ("a1pv", g - 1, 15))
                            rb = 1024 * (i % 2)
                            cv(v, "tensor_copy",
                               vg[i][:].rearrange("p (h j) -> p h j", j=65)[:, :, 0:64],
                               ps_big[:, rb:rb + 1024].rearrange("p (h j) -> p h j", j=64))
                            mark(("a1ve", g, i), soV)
                        if g + 1 < NG:
                            stats(g + 1)
                        qlo = 128 * g
                        for h in range(H):
                            wg(v, soA, ("a1exp", g, h))
                            for k in range(4):
                                cv(v, "tensor_mul", PT[h % 2][k][:, qlo:qlo + 128],
                                   PT[h % 2][k][:, qlo:qlo + 128], maskb[:, k * 128:(k + 1) * 128])
                            mark(("a1msk", g, h), soV)
                            wg(v, soT, ("a1pv", g, h))
                            hc, r0 = h // 2, 64 * (h % 2)
                            dstap = OTacc[r0:r0 + 64, hc * TOWN + qlo:(hc + 1) * TOWN]
                            if g == 0:
                                cv(v, "tensor_copy", dstap, ps_big[0:64, qlo:TOWN])
                            else:
                                cv(v, "tensor_add", dstap, dstap, ps_big[0:64, qlo:TOWN])
                            for li, (a, b) in enumerate(_lspans(g)):
                                nl = nld_tab[(g, h, li)]
                                v.wait_ge(sldma.h, 16 * nl)
                                cv(v, "tensor_copy", tmpl[0:1, 0:b - a], ps_big[64:65, a:b])
                                mark(("a1lev", g, h, li), soV)
                            mark(("a1ote", g, h), soV)
                        if g + 1 < NG:
                            xnbs(g + 1)

                @blk.tensor
                def _(t):
                    t.wait_ge(sw.h, 64)

                    def kv(j):
                        for i in range(4):
                            wg(t, soV, ("a1xnb", 4 * j + i))
                            for b_ in range(8):
                                ntr = 32 * j + 8 * i + b_
                                if ntr >= 2:
                                    wg(t, soA, ("a1te", ntr - 2))
                                soT.inc(t.transpose(ps_tr1[ntr % 2][:, 0:128],
                                                    xnb1[(4 * j + i) % 2][:, b_ * 128:(b_ + 1) * 128], id_b[:]))
                                mark(("a1trj", ntr), soT)
                            mark(("a1tr", j, i), soT)
                        wg(t, soA, ("a1te", 32 * j + 31))
                        if j >= 1:
                            wg(t, soV, ("a1ote", j - 1, 15))
                        for p8 in range(8):
                            rb = (p8 % 2) * 512
                            if p8 >= 2:
                                wg(t, soA, ("a1kt", j, p8 - 2))
                            ins = None
                            for cd in range(8):
                                ins = t.matmul(ps_big[:, rb:rb + 512],
                                               wq_kv[:, cd * 2048 + p8 * 128:cd * 2048 + (p8 + 1) * 128],
                                               xnt1[:, cd * 512:(cd + 1) * 512],
                                               start=(cd == 0), stop=(cd == 7))
                            soT.inc(ins)
                            mark(("a1ktm", j, p8), soT)
                        wg(t, soA, ("a1kt", j, 7))
                        for i in range(4):
                            rb = 1024 * (i % 2)
                            if i >= 2:
                                wg(t, soV, ("a1ve", j, i - 2))
                            ins = None
                            for cd in range(8):
                                lhsT = xnt1[:, cd * 512 + i * 128:cd * 512 + i * 128 + 128]
                                ins = t.matmul(ps_big[:, rb:rb + 512], lhsT,
                                               wq_kv[:, cd * 2048 + 1024:cd * 2048 + 1536],
                                               start=(cd == 0), stop=(cd == 7))
                                ins = t.matmul(ps_big[:, rb + 512:rb + 1024], lhsT,
                                               wq_kv[:, cd * 2048 + 1536:cd * 2048 + 2048],
                                               start=(cd == 0), stop=(cd == 7))
                            soT.inc(ins)
                            mark(("a1vm", j, i), soT)

                    nsc = [0]

                    def scores(g, h):
                        gp, r0 = h // 2, 64 * (h % 2)
                        for k in range(4):
                            for (a, b) in _pieces(g):
                                jj = nsc[0]
                                nsc[0] += 1
                                if jj >= 2:
                                    wg(t, soA, ("a1expj", jj - 2))
                                ins = t.matmul(ps_s[:, (jj % 2) * 512:(jj % 2) * 512 + (b - a)],
                                               ktg[r0:r0 + 64, gp * 512 + k * 128:gp * 512 + (k + 1) * 128],
                                               QT[gp][r0:r0 + 64, a:b], start=True, stop=True)
                                soT.inc(ins)
                                mark(("a1smm", jj), soT)

                    def pv(g, h):
                        wg(t, soV, ("a1msk", g, h))
                        if h >= 1:
                            wg(t, soV, ("a1ote", g, h - 1))
                        ins = None
                        for (a, b) in _pieces(g):
                            for k in range(4):
                                ins = t.matmul(ps_big[0:65, a:b], vg[k][:, h * 65:(h + 1) * 65],
                                               PT[h % 2][k][:, a:b], start=(k == 0), stop=(k == 3))
                        soT.inc(ins)
                        mark(("a1pv", g, h), soT)

                    kv(0)
                    for g in range(NG):
                        scores(g, 0)
                        for h in range(H):
                            if h + 1 < H:
                                scores(g, h + 1)
                            pv(g, h)
                        if g + 1 < NG:
                            kv(g + 1)

                @blk.scalar
                def _(s):
                    def rstd_chain(g):
                        wg(s, soV, ("a1var", g))
                        for i in range(4):
                            cs(s, "activation", c1["lnv"][:, i:i + 1], mv1[:, 2 * i + 1:2 * i + 2], AF.Ln, bias=eps5[:])
                        for i in range(4):
                            cs(s, "activation", c1["rstd"][:, i:i + 1], c1["lnv"][:, i:i + 1], AF.Exp, scale=-0.5)
                        mark(("a1rstd", g), soA)

                    def kv_evac(j):
                        for i in range(4):
                            for b_ in range(8):
                                ntr = 32 * j + 8 * i + b_
                                wg(s, soT, ("a1trj", ntr))
                                if j >= 1 and i == 0 and b_ == 0:
                                    wg(s, soT, ("a1vm", j - 1, 3))
                                cs(s, "activation",
                                   xnt1[:, b_ * 512 + i * 128:b_ * 512 + i * 128 + 128],
                                   ps_tr1[ntr % 2][:, 0:128], AF.Copy)
                                mark(("a1te", ntr), soA)
                        for p8 in range(8):
                            rb = (p8 % 2) * 512
                            wg(s, soT, ("a1ktm", j, p8))
                            if j >= 1 and p8 == 0:
                                wg(s, soT, ("a1smm", scg[j - 1] - 1))
                            cs(s, "activation", ktg[:, p8 * 512:(p8 + 1) * 512],
                               ps_big[:, rb:rb + 512], AF.Copy)
                            mark(("a1kt", j, p8), soA)

                    rstd_chain(0)
                    kv_evac(0)
                    nsc = 0
                    for g in range(NG):
                        if g + 1 < NG:
                            rstd_chain(g + 1)
                        for h in range(H):
                            if h >= 2:
                                wg(s, soT, ("a1pv", g, h - 2))
                            elif g >= 1:
                                wg(s, soT, ("a1pv", g - 1, 14 + h))
                            for k in range(4):
                                for (a, b) in _pieces(g):
                                    wg(s, soT, ("a1smm", nsc))
                                    cs(s, "activation", PT[h % 2][k][:, a:b],
                                       ps_s[:, (nsc % 2) * 512:(nsc % 2) * 512 + (b - a)],
                                       AF.Exp, scale=SC)
                                    mark(("a1expj", nsc), soA)
                                    nsc += 1
                            mark(("a1exp", g, h), soA)
                        if g + 1 < NG:
                            kv_evac(g + 1)

        ctxB.close()
        with ExitStack() as woc:
            Ew = woc.enter_context
            OTb = Ew(nc.sbuf_tensor("OTb", [128, 8 * TOWN], BF16))
            on16 = Ew(nc.sbuf_tensor("on16", [16, 128], F32))
            wos = Ew(nc.sbuf_tensor("wos", [128, 8 * D], BF16))
            lts = [Ew(nc.sbuf_tensor(f"lts{i}", [16, TOWN], F32)) for i in range(2)]
            rlb = Ew(nc.sbuf_tensor("rlb", [128, TOWN], F32))
            xo = [Ew(nc.sbuf_tensor(f"xo{i}", [128, D], F32)) for i in range(2)]
            axt = [Ew(nc.sbuf_tensor(f"axt{i}", [128, D], F32)) for i in range(2)]
            ps_lb = Ew(nc.psum_tensor("ps_lb", [128, TOWN], F32))
            ps_wo = [Ew(nc.psum_tensor(f"ps_wo{i}", [128, D], F32)) for i in range(2)]

            A0SP = ssp.n

            with nc.Block("wo") as blk:
                @blk.gpsimd
                def _(g_):
                    sw.inc(g_.dma_start(wos[:].rearrange("p (c j) -> p c j", c=8),
                                        w_o_in.rearrange("(c p) j -> p c j", p=128)), 16)

                @blk.sync
                def _(sy):
                    sy.wait_ge(sldma.h, 16 * TOT_L)
                    for h in range(H):
                        if h >= 2:
                            wg(sy, soT, ("wolb", h - 2))
                        slt[h % 2].inc(sy.dma_start(lts[h % 2][:], l_tab2[h * NG:(h + 1) * NG, :]), 16)
                    for tc in range(NCH):
                        if tc >= 2:
                            wg(sy, soV, ("woax", tc - 2))
                        sxo[tc % 2].inc(sy.dma_start(xo[tc % 2][:], x_own[tc * 128:(tc + 1) * 128, :]), 16)
                        if tc >= 2:
                            wg(sy, soV, ("woax", tc - 2))
                            sat[tc % 2].inc(sy.dma_start(attnx_sp[(tc - 2) * 128:(tc - 1) * 128, :],
                                                         axt[tc % 2][:]), 16)
                    for tc in range(NCH - 2, NCH):
                        wg(sy, soV, ("woax", tc))
                        sat[tc % 2].inc(sy.dma_start(attnx_sp[tc * 128:(tc + 1) * 128, :], axt[tc % 2][:]), 16)
                    if dbg:
                        wg(sy, soV, ("wootb", H - 1))
                        ssp.inc(sy.dma_start(d_ot, OTacc[:]), 16)
                    sy.wait_ge(ssp.h, ssp.n)
                    sy.wait_ge(sat[0].h, sat[0].n)
                    sy.wait_ge(sat[1].h, sat[1].n)

                @blk.tensor
                def _(t):
                    t.wait_ge(sw.h, 80)
                    wg(t, soV, "on16")
                    for h in range(H):
                        t.wait_ge(slt[h % 2].h, 16 * (h // 2 + 1))
                        if h >= 1:
                            wg(t, soV, ("worl", h - 1))
                        ins = None
                        for pc in range(4):
                            ins = t.matmul(ps_lb[:, pc * 512:(pc + 1) * 512], on16[:],
                                           lts[h % 2][:, pc * 512:(pc + 1) * 512], start=True, stop=True)
                        soT.inc(ins)
                        mark(("wolb", h), soT)
                    for tc in range(NCH):
                        wg(t, soV, ("wootb", H - 1))
                        if tc >= 2:
                            wg(t, soV, ("woax", tc - 2))
                        ins = None
                        for cd in range(8):
                            lhsT = OTb[:, cd * TOWN + tc * 128:cd * TOWN + (tc + 1) * 128]
                            ins = t.matmul(ps_wo[tc % 2][:, 0:512], lhsT, wos[:, cd * D:cd * D + 512],
                                           start=(cd == 0), stop=(cd == 7))
                            ins = t.matmul(ps_wo[tc % 2][:, 512:1024], lhsT, wos[:, cd * D + 512:(cd + 1) * D],
                                           start=(cd == 0), stop=(cd == 7))
                        soT.inc(ins)
                        mark(("womm", tc), soT)

                @blk.vector
                def _(v):
                    cv(v, "memset", on16[:], 1.0)
                    mark("on16", soV)
                    for h in range(H):
                        wg(v, soT, ("wolb", h))
                        cv(v, "reciprocal", rlb[:], ps_lb[:])
                        mark(("worl", h), soV)
                        hc, r0 = h // 2, 64 * (h % 2)
                        cv(v, "tensor_mul", OTb[r0:r0 + 64, hc * TOWN:(hc + 1) * TOWN],
                           OTacc[r0:r0 + 64, hc * TOWN:(hc + 1) * TOWN], rlb[r0:r0 + 64, :])
                        mark(("wootb", h), soV)
                    for tc in range(NCH):
                        wg(v, soT, ("womm", tc))
                        v.wait_ge(sxo[tc % 2].h, 16 * (tc // 2 + 1))
                        if tc >= 2:
                            v.wait_ge(sat[tc % 2].h, 16 * ((tc - 2) // 2 + 1))
                        cv(v, "tensor_add", axt[tc % 2][:], xo[tc % 2][:], ps_wo[tc % 2][:])
                        mark(("woax", tc), soV)

    A1SP = ssp.n
    with ExitStack() as mctx:
        Em = mctx.enter_context
        y_acc = Em(nc.sbuf_tensor("y_acc", [128, NCH * D], F32))
        X2 = Em(nc.sbuf_tensor("mX2", [128, 8 * TOWN], BF16))
        hsl = Em(nc.sbuf_tensor("mh", [128, 8 * TOWN], BF16))
        w2s = [Em(nc.sbuf_tensor(f"mw2{i}", [128, 8 * D], BF16)) for i in range(2)]
        w1t = [Em(nc.sbuf_tensor(f"mw1{i}", [128, 8 * 128], BF16)) for i in range(2)]
        b1s = Em(nc.sbuf_tensor("mb1", [128, NE * 32], F32))
        b2s = Em(nc.sbuf_tensor("mb2", [8, D], F32))
        coefT = Em(nc.sbuf_tensor("mcoefT", [8, NCH * 128], F32))
        coef = Em(nc.sbuf_tensor("mcoef", [128, NCH * NE], F32))
        axt2 = [Em(nc.sbuf_tensor(f"max{i}", [128, D], F32)) for i in range(2)]
        outt = [Em(nc.sbuf_tensor(f"mo{i}", [128, D], F32)) for i in range(2)]
        ps_h = [Em(nc.psum_tensor(f"mph{i}", [128, 1024], F32)) for i in range(2)]
        ps_y = [Em(nc.psum_tensor(f"mpy{i}", [128, 1024], F32)) for i in range(2)]

        with nc.Block("moe") as blk:
            @blk.sync
            def _(sy):
                sy.wait_ge(ssp.h, A1SP)
                sy.wait_ge(sat[0].h, sat[0].n)
                sy.wait_ge(sat[1].h, sat[1].n)
                smc.inc(sy.dma_start(X2[:], xn2t_sp), 16)
                smc.inc(sy.dma_start(b2s[:], b2_in), 16)
                smc.inc(sy.dma_start(b1s[:], b1_t), 16)
                smc.inc(sy.dma_start(coefT[:], coeft_sp), 16)
                smc.inc(sy.dma_start(coef[:], coef_sp), 16)
                for tc in range(NCH):
                    if tc >= 2:
                        wg(sy, soV, ("mfin", tc - 2))
                    sax[tc % 2].inc(sy.dma_start(axt2[tc % 2][:], attnx_sp[tc * 128:(tc + 1) * 128, :]), 16)
                    if tc >= 2:
                        wg(sy, soV, ("mfin", tc - 2))
                        sot[tc % 2].inc(sy.dma_start(out_t[(tc - 2) * 128:(tc - 1) * 128, :], outt[tc % 2][:]), 16)
                for tc in range(NCH - 2, NCH):
                    wg(sy, soV, ("mfin", tc))
                    sot[tc % 2].inc(sy.dma_start(out_t[tc * 128:(tc + 1) * 128, :], outt[tc % 2][:]), 16)
                sy.wait_ge(sot[0].h, 16 * (NCH // 2))
                sy.wait_ge(sot[1].h, 16 * (NCH // 2))

            @blk.gpsimd
            def _(g_):
                ti = 0
                for e in range(NE):
                    for fq in range(4):
                        q = e * 4 + fq
                        if q >= 2:
                            wg(g_, soT, ("mym", q - 2, NCH - 1))
                        r0 = e * FF + fq * 1024
                        sw2[q % 2].inc(g_.dma_start(w2s[q % 2][:].rearrange("p (c j) -> p c j", c=8),
                                                    w2_2d[r0:r0 + 1024, :].rearrange("(c p) j -> p c j", p=128)), 16)
                        for fc in range(8):
                            if ti >= 2:
                                wg(g_, soT, ("mhm", ti - 2, 1))
                            cl = fq * 1024 + fc * 128
                            sw1[ti % 2].inc(g_.dma_start(
                                w1t[ti % 2][:].rearrange("p (c f) -> p c f", c=8),
                                w1_2d[e * D:(e + 1) * D, cl:cl + 128].rearrange("(c p) f -> p c f", p=128)), 16)
                            ti += 1

            @blk.tensor
            def _(t):
                t.wait_ge(smc.h, 80)
                for tc in range(NCH):
                    if tc >= 2:
                        wg(t, soV, ("mye", -1, tc - 2))
                    ins = t.matmul(ps_y[tc % 2][:, 0:512], coefT[:, tc * 128:(tc + 1) * 128],
                                   b2s[:, 0:512], start=True, stop=True)
                    ins = t.matmul(ps_y[tc % 2][:, 512:1024], coefT[:, tc * 128:(tc + 1) * 128],
                                   b2s[:, 512:1024], start=True, stop=True)
                    soT.inc(ins)
                    mark(("myi", tc), soT)
                ti = 0
                for e in range(NE):
                    for fq in range(4):
                        q = e * 4 + fq
                        for fc in range(8):
                            t.wait_ge(sw1[ti % 2].h, 16 * (ti // 2 + 1))
                            for sub in range(2):
                                if ti > 0:
                                    wg(t, soA, ("msi", ti - 1, sub))
                                ins = None
                                for cd in range(8):
                                    co = cd * TOWN + sub * 1024
                                    ins = t.matmul(ps_h[sub][:, 0:512], w1t[ti % 2][:, cd * 128:(cd + 1) * 128],
                                                   X2[:, co:co + 512], start=(cd == 0), stop=(cd == 7))
                                    ins = t.matmul(ps_h[sub][:, 512:1024], w1t[ti % 2][:, cd * 128:(cd + 1) * 128],
                                                   X2[:, co + 512:co + 1024], start=(cd == 0), stop=(cd == 7))
                                soT.inc(ins)
                                mark(("mhm", ti, sub), soT)
                            ti += 1
                        wg(t, soA, ("msi", q * 8 + 7, 1))
                        t.wait_ge(sw2[q % 2].h, 16 * (q // 2 + 1))
                        for tc in range(NCH):
                            if tc >= 2:
                                wg(t, soV, ("mye", q, tc - 2))
                            elif q >= 1:
                                wg(t, soV, ("mye", q - 1, 14 + tc))
                            else:
                                wg(t, soV, ("mye", -1, 14 + tc % 2))
                            ins = None
                            for fc in range(8):
                                lhsT = hsl[:, fc * TOWN + tc * 128:fc * TOWN + (tc + 1) * 128]
                                ins = t.matmul(ps_y[tc % 2][:, 0:512], lhsT, w2s[q % 2][:, fc * D:fc * D + 512],
                                               start=(fc == 0), stop=(fc == 7))
                                ins = t.matmul(ps_y[tc % 2][:, 512:1024], lhsT,
                                               w2s[q % 2][:, fc * D + 512:(fc + 1) * D],
                                               start=(fc == 0), stop=(fc == 7))
                            soT.inc(ins)
                            mark(("mym", q, tc), soT)

            @blk.scalar
            def _(s):
                ti = 0
                for e in range(NE):
                    for fq in range(4):
                        q = e * 4 + fq
                        for fc in range(8):
                            if q >= 1 and fc == 0:
                                wg(s, soT, ("mym", q - 1, NCH - 1))
                            for sub in range(2):
                                wg(s, soT, ("mhm", ti, sub))
                                cs(s, "activation", hsl[:, fc * TOWN + sub * 1024:fc * TOWN + sub * 1024 + 1024],
                                   ps_h[sub][:], AF.Silu,
                                   bias=b1s[:, e * 32 + fq * 8 + fc:e * 32 + fq * 8 + fc + 1])
                                mark(("msi", ti, sub), soA)
                            ti += 1

            @blk.vector
            def _(v):
                for tc in range(NCH):
                    wg(v, soT, ("myi", tc))
                    cv(v, "tensor_copy", y_acc[:, tc * D:(tc + 1) * D], ps_y[tc % 2][:])
                    mark(("mye", -1, tc), soV)
                for e in range(NE):
                    for fq in range(4):
                        q = e * 4 + fq
                        for tc in range(NCH):
                            wg(v, soT, ("mym", q, tc))
                            cv(v, "scalar_tensor_tensor",
                               y_acc[:, tc * D:(tc + 1) * D], ps_y[tc % 2][:],
                               coef[:, tc * NE + e:tc * NE + e + 1],
                               y_acc[:, tc * D:(tc + 1) * D], ALU.mult, ALU.add)
                            mark(("mye", q, tc), soV)
                for tc in range(NCH):
                    v.wait_ge(sax[tc % 2].h, 16 * (tc // 2 + 1))
                    if tc >= 2:
                        v.wait_ge(sot[tc % 2].h, 16 * ((tc - 2) // 2 + 1))
                    cv(v, "tensor_add", outt[tc % 2][:], axt2[tc % 2][:], y_acc[:, tc * D:(tc + 1) * D])
                    mark(("mfin", tc), soV)

    ctx.close()
    return nc


def build_nc(dbg=False, upto="full", sub=""):
    m1 = {}
    _build(dbg, upto, {}, m1, sub)
    m2 = {}
    nc = _build(dbg, upto, m1, m2, sub)
    assert m1 == m2, "nondeterministic build marks"
    return nc


_CACHE = {}


def _host_prep(inputs):
    x = np.asarray(inputs["L0_attn_x"], dtype=np.float32)
    wqkv = np.asarray(inputs["L0_attn_w_qkv"], dtype=np.float32)
    idx = np.arange(3 * D).reshape(H, 3, 64)
    perm = np.concatenate([idx[:, 0].ravel(), idx[:, 1].ravel(), idx[:, 2].ravel()])
    wqkv_r = np.ascontiguousarray(wqkv[:, perm])
    w_o = np.ascontiguousarray(np.asarray(inputs["L0_attn_w_o"], dtype=np.float32))
    rms_bc = np.ascontiguousarray(np.broadcast_to(
        np.asarray(inputs["L0_ffn_rms_w"], dtype=np.float32), (128, D)))
    gate_t = np.ascontiguousarray(np.asarray(inputs["L0_ffn_gate_w"], dtype=np.float32).T)
    b1 = np.asarray(inputs["L0_ffn_b1"], dtype=np.float32)
    b1_t = np.ascontiguousarray(b1.reshape(NE, 32, 128).transpose(2, 0, 1).reshape(128, NE * 32))
    b2 = np.ascontiguousarray(np.asarray(inputs["L0_ffn_b2"], dtype=np.float32))
    w1 = np.ascontiguousarray(np.asarray(inputs["L0_ffn_W1"], dtype=np.float32).reshape(NE * D, FF))
    w2 = np.ascontiguousarray(np.asarray(inputs["L0_ffn_W2"], dtype=np.float32).reshape(NE * FF, D))
    ident = np.eye(128, dtype=np.float32)
    tri = np.triu(np.ones((128, 128), dtype=np.float32))
    in_maps = []
    rows_all = []
    for c in range(8):
        b_, cc = c // 4, c % 4
        rows = np.concatenate([np.arange((4 * g + cc) * 128, (4 * g + cc) * 128 + 128)
                               for g in range(NG)])
        rows_all.append((b_, rows))
        mk = np.zeros((4, 128, 128), dtype=np.float32)
        for r in range(4):
            if r < cc:
                mk[r] = 1.0
            elif r == cc:
                mk[r] = tri
        in_maps.append(dict(
            x_own=np.ascontiguousarray(x[b_].reshape(S, D)[rows]),
            x_str=np.ascontiguousarray(x[b_]),
            wqkv=wqkv_r, w_o_in=w_o, rms_bc=rms_bc, gate_t=gate_t, b1_t=b1_t,
            b2_in=b2, w1_2d=w1, w2_2d=w2, masks=np.ascontiguousarray(mk), ident=ident))
    return in_maps, rows_all


def kernel(**inputs):
    if "nc" not in _CACHE:
        _CACHE["nc"] = build_nc(dbg=False, upto="full")
    nc = _CACHE["nc"]
    in_maps, rows_all = _host_prep(inputs)
    res = run_bass_kernel_spmd(nc, in_maps, core_ids=list(range(8)))
    out = np.empty((B, S, D), dtype=np.float32)
    for c in range(8):
        b_, rows = rows_all[c]
        out[b_].reshape(S, D)[rows] = res.results[c]["out"]
    return out


# version 03
# speedup vs baseline: 1.2423x; 1.2423x over previous
import numpy as np
import ml_dtypes
import concourse.bass as bass
import concourse.mybir as mybir
from concourse.bass_utils import run_bass_kernel_spmd
from contextlib import ExitStack

F32 = mybir.dt.float32
BF16 = mybir.dt.bfloat16
AF = mybir.ActivationFunctionType
ALU = mybir.AluOpType
AX = mybir.AxisListType

B, S, D = 2, 8192, 1024
H = 16
FF, NE = 4096, 8
TOWN, NCH = 2048, 16
NG, GSZ = 16, 4
NPAIR = 8
SC = 1.0 / 32.0
EPS_LN, EPS_RMS = 1e-5, 1e-6


def _pieces(g):
    qlo = 128 * g
    out = []
    p = qlo
    fe = min(2048, (qlo // 512 + 1) * 512)
    if fe > p:
        out.append((p, fe))
        p = fe
    while p < 2048:
        out.append((p, p + 512))
        p += 512
    return out


def _lspans(g):
    qlo = 128 * g
    out = []
    p = qlo
    fe = min(TOWN, (qlo // 256 + 1) * 256)
    if fe > p:
        out.append((p, fe))
        p = fe
    while p < TOWN:
        out.append((p, p + 256))
        p += 256
    return out


class Sem:
    def __init__(self, nc, name):
        self.h = nc.alloc_semaphore(name)
        self.n = 0

    def inc(self, instr, k=1):
        instr.then_inc(self.h, k)
        self.n += k
        return self.n


def _build(dbg, upto, use, rec, sub=""):
    nc = bass.Bass("TRN2", target_bir_lowering=False)
    DT = nc.dram_tensor
    x_own = DT("x_own", [TOWN, D], F32, kind="ExternalInput").ap()
    x_str = DT("x_str", [S, D], F32, kind="ExternalInput").ap()
    wqkv = DT("wqkv", [D, 3 * D], F32, kind="ExternalInput").ap()
    w_o_in = DT("w_o_in", [D, D], F32, kind="ExternalInput").ap()
    rms_bc = DT("rms_bc", [128, D], F32, kind="ExternalInput").ap()
    gate_t = DT("gate_t", [D, NE], F32, kind="ExternalInput").ap()
    b1_t = DT("b1_t", [128, NE * 32], F32, kind="ExternalInput").ap()
    b2_in = DT("b2_in", [NE, D], F32, kind="ExternalInput").ap()
    w1_2d = DT("w1_2d", [NE * D, FF], F32, kind="ExternalInput").ap()
    w2_2d = DT("w2_2d", [NE * FF, D], F32, kind="ExternalInput").ap()
    masks = DT("masks", [4, 128, 128], F32, kind="ExternalInput").ap()
    ident = DT("ident", [128, 128], F32, kind="ExternalInput").ap()
    out_t = DT("out", [TOWN, D], F32, kind="ExternalOutput").ap()
    dkind = "ExternalOutput" if dbg else "Internal"
    xn2t_sp = DT("xn2t_sp", [128, 8 * TOWN], BF16, kind=dkind).ap()
    attnx_sp = DT("attnx_sp", [TOWN, D], F32, kind=dkind).ap()
    l_tab = DT("l_tab", [H, NG, TOWN], F32, kind=dkind).ap()
    l_tab2 = l_tab.rearrange("h g q -> (h g) q")
    if dbg:
        d_qt = DT("d_qt", [128, NPAIR * TOWN], BF16, kind="ExternalOutput").ap()
        d_coef = DT("d_coef", [128, NCH * NE], F32, kind="ExternalOutput").ap()
        d_ot = DT("d_ot", [128, 8 * TOWN], F32, kind="ExternalOutput").ap()

    coeft_sp = DT("coeft_sp", [8, NCH * 128], F32, kind="Internal").ap()
    coef_sp = DT("coef_sp", [128, NCH * NE], F32, kind="Internal").ap()
    ctx = ExitStack()
    E = ctx.enter_context
    maskb = E(nc.sbuf_tensor("maskb", [128, 4 * 128], BF16))
    id_b = E(nc.sbuf_tensor("id_b", [128, 128], BF16))
    eps5 = E(nc.sbuf_tensor("eps5", [128, 1], F32))
    ctxB = ExitStack()
    EB = ctxB.enter_context
    wq_kv = EB(nc.sbuf_tensor("wq_kv", [128, 8 * 2048], BF16, side="right"))
    QT = [EB(nc.sbuf_tensor(f"qt{g}", [128, TOWN], BF16, side="right")) for g in range(NPAIR)]

    soV = Sem(nc, "soV")
    soA = Sem(nc, "soA")
    soT = Sem(nc, "soT")
    sw = Sem(nc, "s_w")
    szf = Sem(nc, "s_zf")
    ssp = Sem(nc, "s_sp")
    sldma = Sem(nc, "s_ldma")
    smc = Sem(nc, "s_mc")
    sout_ = Sem(nc, "s_out")
    sxl = Sem(nc, "s_xl")
    sxsb = [Sem(nc, f"s_xs{i}") for i in range(4)]
    slt = [Sem(nc, f"s_lt{i}") for i in range(2)]
    sxo = [Sem(nc, f"s_xo{i}") for i in range(2)]
    sax = [Sem(nc, f"s_ax{i}") for i in range(2)]
    sw1 = [Sem(nc, f"s_w1{i}") for i in range(2)]
    sw2 = [Sem(nc, f"s_w2{i}") for i in range(2)]
    sat = [Sem(nc, f"s_at{i}") for i in range(2)]
    sot = [Sem(nc, f"s_ot{i}") for i in range(2)]
    slp = [Sem(nc, f"s_lp{i}") for i in range(2)]

    def cv(v, name, *a, **kw):
        v.wait_ge(soV.h, soV.n)
        ins = getattr(v, name)(*a, **kw)
        soV.inc(ins)
        return ins

    def cs(s, name, *a, **kw):
        s.wait_ge(soA.h, soA.n)
        ins = getattr(s, name)(*a, **kw)
        soA.inc(ins)
        return ins

    def mark(key, sem):
        rec[key] = sem.n

    def wg(eng, sem, key):
        eng.wait_ge(sem.h, use.get(key, 0))

    with ExitStack() as sctx:
        zb = sctx.enter_context(nc.sbuf_tensor("zb", [64, 8192], F32))
        with nc.Block("setup") as blk:
            @blk.gpsimd
            def _(g):
                sw.inc(g.dma_start(wq_kv[:].rearrange("p (c j) -> p c j", c=8),
                                   wqkv[:, D:3 * D].rearrange("(c p) j -> p c j", p=128)), 16)
                sw.inc(g.dma_start(maskb[:].rearrange("p (r q) -> p r q", r=4),
                                   masks.rearrange("r k q -> k r q")), 16)
                sw.inc(g.dma_start(id_b[:], ident), 16)

            @blk.vector
            def _(v):
                cv(v, "memset", zb[:], 0.0)
                cv(v, "memset", eps5[:], EPS_LN)
                mark("zb", soV)

            @blk.sync
            def _(sy):
                wg(sy, soV, "zb")
                lt_flat = l_tab.rearrange("h g q -> (h g q)").rearrange("(a b) -> a b", a=64)
                szf.inc(sy.dma_start(lt_flat[:, :], zb[:]), 16)
                sy.wait_ge(szf.h, 16)

    with ExitStack() as actx:
        Ea = actx.enter_context
        wq_q = Ea(nc.sbuf_tensor("wq_q", [128, 8 * D], BF16))
        rmsb = Ea(nc.sbuf_tensor("rmsb", [128, D], F32))
        coefT = Ea(nc.sbuf_tensor("coefT", [8, NCH * 128], F32))
        id_f = Ea(nc.sbuf_tensor("id_f", [128, 128], F32))
        coef = Ea(nc.sbuf_tensor("coef", [128, NCH * NE], F32))
        eps6 = Ea(nc.sbuf_tensor("eps6", [128, 1], F32))
        ghi = Ea(nc.sbuf_tensor("ghi", [128, 8 * NE], BF16))
        glo = Ea(nc.sbuf_tensor("glo", [128, 8 * NE], BF16))
        gf = Ea(nc.sbuf_tensor("gf", [128, 8 * NE], F32))
        xbuf = [Ea(nc.sbuf_tensor(f"a0x{i}", [128, D], F32)) for i in range(4)]
        xnb = [Ea(nc.sbuf_tensor(f"a0n{i}", [128, D], BF16)) for i in range(2)]
        xn2b = [Ea(nc.sbuf_tensor(f"a0n2{i}", [128, D], BF16)) for i in range(2)]
        lob = [Ea(nc.sbuf_tensor(f"a0lo{i}", [128, D], BF16)) for i in range(2)]
        xnf = Ea(nc.sbuf_tensor("a0xnf", [128, D], F32))
        xrf = Ea(nc.sbuf_tensor("a0xrf", [128, D], F32))
        x2f = Ea(nc.sbuf_tensor("a0x2f", [128, D], F32))
        bst = Ea(nc.sbuf_tensor("a0bst", [128, 48], F32))
        mv = Ea(nc.sbuf_tensor("a0mv", [128, 8], F32))
        xnt = Ea(nc.sbuf_tensor("a0xnt", [128, 8 * 512], BF16))
        x2t = Ea(nc.sbuf_tensor("a0x2t", [128, 8 * 512], BF16))
        lot = [Ea(nc.sbuf_tensor(f"a0lot{i}", [128, 8 * 128], BF16)) for i in range(2)]
        cst = {nm: Ea(nc.sbuf_tensor("a0" + nm, [128, 4], F32))
               for nm in "vv rec ms lnv rstd lnm cc".split()}
        g16 = {nm: Ea(nc.sbuf_tensor("a0g" + nm, [128, 16], F32))
               for nm in "gm ngm gs gr".split()}
        Lsb = Ea(nc.sbuf_tensor("a0L", [128, NCH * NE], F32))
        egs = Ea(nc.sbuf_tensor("a0eg", [128, NE], F32))
        ps_tr = [Ea(nc.psum_tensor(f"a0ptr{i}", [128, 1024], BF16)) for i in range(2)]
        ps_q = [Ea(nc.psum_tensor(f"a0pq{i}", [128, 512], F32)) for i in range(2)]
        ps_g = [Ea(nc.psum_tensor(f"a0pg{i}", [128, NE], F32)) for i in range(2)]
        ps_ct = Ea(nc.psum_tensor("a0pct", [8, 128], F32))

        with nc.Block("a0") as blk:
            @blk.gpsimd
            def _(g):
                sw.inc(g.dma_start(wq_q[:].rearrange("p (c j) -> p c j", c=8),
                                   wqkv[:, 0:D].rearrange("(c p) j -> p c j", p=128)), 16)

            @blk.sync
            def _(sy):
                sxl.inc(sy.dma_start(rmsb[:], rms_bc), 16)
                sxl.inc(sy.dma_start(id_f[:], ident), 16)
                sxl.inc(sy.dma_start(gf[:].rearrange("p (c e) -> p c e", c=8),
                                     gate_t.rearrange("(c p) e -> p c e", p=128)), 16)

                def spills(st):
                    if "nospill" in sub:
                        return
                    wg(sy, soA, ("a0te_ch", st * 4 + 3))
                    for cd in range(8):
                        ssp.inc(sy.dma_start(xn2t_sp[:, cd * TOWN + st * 512:cd * TOWN + (st + 1) * 512],
                                             x2t[:, cd * 512:(cd + 1) * 512]), 16)
                for st in range(4):
                    for i in range(4):
                        ch = st * 4 + i
                        if ch >= 4:
                            wg(sy, soV, ("a0ln", ch - 4))
                        sxsb[ch % 4].inc(sy.dma_start(xbuf[ch % 4][:], x_own[ch * 128:(ch + 1) * 128, :]), 16)
                    if st >= 1:
                        spills(st - 1)
                spills(3)

            @blk.vector
            def _(v):
                c = cst
                cv(v, "memset", eps6[:], EPS_RMS)
                v.wait_ge(sxl.h, 48)
                cv(v, "tensor_copy", ghi[:], gf[:])
                cv(v, "scalar_tensor_tensor", glo[:], ghi[:], -1.0, gf[:], ALU.mult, ALU.add)
                mark("gate_hl", soV)
                for st in range(4):
                    for i in range(4):
                        ch = st * 4 + i
                        v.wait_ge(sxsb[ch % 4].h, 16 * (ch // 4 + 1))
                        cv(v, "bn_stats", bst[:, i * 12:i * 12 + 6], xbuf[ch % 4][:, 0:512])
                        cv(v, "bn_stats", bst[:, i * 12 + 6:i * 12 + 12], xbuf[ch % 4][:, 512:1024])
                    for i in range(4):
                        cv(v, "bn_aggr", mv[:, i * 2:i * 2 + 2], bst[:, i * 12:i * 12 + 12])
                    for i in range(4):
                        cv(v, "tensor_scalar", c["vv"][:, i:i + 1], mv[:, 2 * i + 1:2 * i + 2],
                           EPS_LN, None, ALU.add)
                    for i in range(4):
                        cv(v, "reciprocal", c["rec"][:, i:i + 1], c["vv"][:, i:i + 1])
                    for i in range(4):
                        cv(v, "tensor_mul", c["ms"][:, i:i + 1], mv[:, 2 * i + 1:2 * i + 2],
                           c["rec"][:, i:i + 1])
                    mark(("a0var", st), soV)
                    wg(v, soA, ("a0rstd", st))
                    for i in range(4):
                        ch = st * 4 + i
                        if ch >= 2:
                            wg(v, soT, ("a0tr", ch - 2))
                        cv(v, "tensor_scalar", xnf[:], xbuf[ch % 4][:], mv[:, 2 * i:2 * i + 1],
                           c["rstd"][:, i:i + 1], ALU.subtract, ALU.mult)
                        cv(v, "tensor_copy", xnb[ch % 2][:], xnf[:])
                        cv(v, "tensor_mul", xrf[:], xnf[:], rmsb[:])
                        cv(v, "tensor_scalar", x2f[:], xrf[:], c["cc"][:, i:i + 1], None, ALU.mult)
                        cv(v, "tensor_copy", xn2b[ch % 2][:], x2f[:])
                        cv(v, "scalar_tensor_tensor", lob[ch % 2][:], xn2b[ch % 2][:], -1.0, x2f[:],
                           ALU.mult, ALU.add)
                        mark(("a0ln", ch), soV)

            @blk.tensor
            def _(t):
                if "notr" in sub:
                    return
                t.wait_ge(sw.h, 64)
                wg(t, soV, "gate_hl")
                for st in range(4):
                    for i in range(4):
                        ch = st * 4 + i
                        wg(t, soV, ("a0ln", ch))
                        for sj, src in enumerate((xnb[ch % 2], xn2b[ch % 2], lob[ch % 2])):
                            for j in range(8):
                                ntr = 24 * ch + 8 * sj + j
                                if ntr >= 2:
                                    wg(t, soA, ("a0te", ntr - 2))
                                soT.inc(t.transpose(ps_tr[ntr % 2][:, 0:128],
                                                    src[:, j * 128:(j + 1) * 128], id_b[:]))
                                mark(("a0trj", ntr), soT)
                        mark(("a0tr", ch), soT)
                        if "nogate" in sub:
                            continue
                        wg(t, soA, ("a0te_ch", ch))
                        if ch >= 2:
                            wg(t, soA, ("a0gl", ch - 2))
                        nmm = 0
                        ins = None
                        for cd in range(8):
                            for src2 in range(3):
                                if src2 == 0:
                                    lhsT, rh = x2t[:, cd * 512 + i * 128:cd * 512 + i * 128 + 128], ghi
                                elif src2 == 1:
                                    lhsT, rh = x2t[:, cd * 512 + i * 128:cd * 512 + i * 128 + 128], glo
                                else:
                                    lhsT, rh = lot[ch % 2][:, cd * 128:(cd + 1) * 128], ghi
                                nmm += 1
                                ins = t.matmul(ps_g[ch % 2][:], lhsT, rh[:, cd * NE:(cd + 1) * NE],
                                               start=(nmm == 1), stop=(nmm == 24))
                        soT.inc(ins)
                        mark(("a0gate", ch), soT)
                    for gp in range(8):
                        if "noqt" in sub:
                            break
                        k = st * 8 + gp
                        if k >= 2:
                            wg(t, soA, ("a0qe", k - 2))
                        ins = None
                        for cd in range(8):
                            ins = t.matmul(ps_q[k % 2][:], wq_q[:, cd * D + gp * 128:cd * D + (gp + 1) * 128],
                                           xnt[:, cd * 512:(cd + 1) * 512], start=(cd == 0), stop=(cd == 7))
                        soT.inc(ins)
                        mark(("a0qt", k), soT)
                if "nogate" not in sub:
                    wg(t, soV, ("a0coef", NCH - 1))
                    for ch in range(NCH):
                        if ch >= 1:
                            wg(t, soV, ("a0ctv", ch - 1))
                        soT.inc(t.transpose(ps_ct[:], coef[:, ch * NE:(ch + 1) * NE], id_f[:]))
                        mark(("a0ct", ch), soT)

            @blk.scalar
            def _(s):
                c = cst
                for st in range(4):
                    wg(s, soV, ("a0var", st))
                    for i in range(4):
                        cs(s, "activation", c["lnv"][:, i:i + 1], mv[:, 2 * i + 1:2 * i + 2], AF.Ln, bias=eps5[:])
                    for i in range(4):
                        cs(s, "activation", c["lnm"][:, i:i + 1], c["ms"][:, i:i + 1], AF.Ln, bias=eps6[:])
                    for i in range(4):
                        cs(s, "activation", c["rstd"][:, i:i + 1], c["lnv"][:, i:i + 1], AF.Exp, scale=-0.5)
                    for i in range(4):
                        cs(s, "activation", c["cc"][:, i:i + 1], c["lnm"][:, i:i + 1], AF.Exp, scale=-0.5)
                    mark(("a0rstd", st), soA)
                    if "notr" in sub or "noev" in sub:
                        continue
                    for i in range(4):
                        ch = st * 4 + i
                        if ch >= 2:
                            wg(s, soT, ("a0gate", ch - 2))
                        if i == 0 and st >= 1:
                            wg(s, soT, ("a0qt", st * 8 - 1))
                            s.wait_ge(ssp.h, 128 * st)
                        for sj, dst in enumerate(("xnt", "x2t", "lot")):
                            for j in range(8):
                                ntr = 24 * ch + 8 * sj + j
                                wg(s, soT, ("a0trj", ntr))
                                psl = ps_tr[ntr % 2][:, 0:128]
                                if dst == "xnt":
                                    cs(s, "activation", xnt[:, j * 512 + i * 128:j * 512 + i * 128 + 128],
                                       psl, AF.Copy)
                                elif dst == "x2t":
                                    cs(s, "activation", x2t[:, j * 512 + i * 128:j * 512 + i * 128 + 128],
                                       psl, AF.Copy)
                                else:
                                    cs(s, "activation", lot[ch % 2][:, j * 128:(j + 1) * 128], psl, AF.Copy)
                                mark(("a0te", ntr), soA)
                        mark(("a0te_ch", ch), soA)
                        if "nogate" not in sub:
                            wg(s, soT, ("a0gate", ch))
                            cs(s, "activation", Lsb[:, ch * NE:(ch + 1) * NE], ps_g[ch % 2][:], AF.Copy)
                            mark(("a0gl", ch), soA)
                    for gp in range(8):
                        if "noqt" in sub:
                            break
                        k = st * 8 + gp
                        wg(s, soT, ("a0qt", k))
                        cs(s, "activation", QT[gp][:, st * 512:(st + 1) * 512], ps_q[k % 2][:], AF.Copy)
                        mark(("a0qe", k), soA)
                if "notr" not in sub and "nogate" not in sub:
                    for ch in range(NCH):
                        wg(s, soV, ("a0ngm", ch))
                        cs(s, "activation", egs[:], Lsb[:, ch * NE:(ch + 1) * NE], AF.Exp,
                           bias=g16["ngm"][:, ch:ch + 1], accum_out=g16["gs"][:, ch:ch + 1])
                        mark(("a0ga", ch), soA)

            @blk.vector
            def _(v):
                if "notr" in sub or "nogate" in sub:
                    return
                for ch in range(NCH):
                    wg(v, soA, ("a0gl", ch))
                    cv(v, "reduce_max", g16["gm"][:, ch:ch + 1], Lsb[:, ch * NE:(ch + 1) * NE], axis=AX.X)
                for ch in range(NCH):
                    cv(v, "tensor_scalar", g16["ngm"][:, ch:ch + 1], g16["gm"][:, ch:ch + 1],
                       -1.0, None, ALU.mult)
                    mark(("a0ngm", ch), soV)
                for ch in range(NCH):
                    wg(v, soA, ("a0ga", ch))
                    cv(v, "reciprocal", g16["gr"][:, ch:ch + 1], g16["gs"][:, ch:ch + 1])
                for ch in range(NCH):
                    cv(v, "tensor_scalar", coef[:, ch * NE:(ch + 1) * NE],
                       Lsb[:, ch * NE:(ch + 1) * NE],
                       g16["gm"][:, ch:ch + 1], g16["gr"][:, ch:ch + 1], ALU.is_equal, ALU.mult)
                    mark(("a0coef", ch), soV)
                for ch in range(NCH):
                    wg(v, soT, ("a0ct", ch))
                    cv(v, "tensor_copy", coefT[:, ch * 128:(ch + 1) * 128], ps_ct[:])
                    mark(("a0ctv", ch), soV)

            @blk.sync
            def _(sy):
                if "nogate" not in sub:
                    wg(sy, soV, ("a0ctv", NCH - 1))
                    ssp.inc(sy.dma_start(coeft_sp, coefT[:]), 16)
                    wg(sy, soV, ("a0coef", NCH - 1))
                    ssp.inc(sy.dma_start(coef_sp, coef[:]), 16)
                if dbg and "notr" not in sub:
                    wg(sy, soA, ("a0qe", 31))
                    for gp in range(NPAIR):
                        ssp.inc(sy.dma_start(d_qt[:, gp * TOWN:(gp + 1) * TOWN], QT[gp][:]), 16)
                    wg(sy, soV, ("a0coef", NCH - 1))
                    ssp.inc(sy.dma_start(d_coef, coef[:]), 16)
                sy.wait_ge(ssp.h, ssp.n)

    if upto == "a0":
        with nc.Block("fin0") as blk:
            @blk.sync
            def _(sy):
                sout_.inc(sy.dma_start(out_t, x_own), 16)
                sy.wait_ge(sout_.h, 16)
        ctx.close()
        return nc

    scg = []
    tot = 0
    for g in range(NG):
        tot += H * 4 * len(_pieces(g))
        scg.append(tot)

    with ExitStack() as wctx:
        OTacc = wctx.enter_context(nc.sbuf_tensor("OTacc", [128, 8 * TOWN], F32))
        with ExitStack() as a1c:
            E1 = a1c.enter_context
            xs = [E1(nc.sbuf_tensor(f"a1x{i}", [128, D], F32)) for i in range(4)]
            xnb1 = [E1(nc.sbuf_tensor(f"a1n{i}", [128, D], BF16)) for i in range(2)]
            bst1 = E1(nc.sbuf_tensor("a1bst", [128, 48], F32))
            mv1 = E1(nc.sbuf_tensor("a1mv", [128, 8], F32))
            xnt1 = E1(nc.sbuf_tensor("a1xnt", [128, 8 * 512], BF16))
            ktg = E1(nc.sbuf_tensor("a1kt", [128, 8 * 512], BF16))
            vg = [E1(nc.sbuf_tensor(f"a1v{i}", [128, H * 65], BF16)) for i in range(4)]
            PT = [[E1(nc.sbuf_tensor(f"a1pt{s0}_{k}", [128, TOWN], BF16)) for k in range(4)]
                  for s0 in range(2)]
            tmpl = [E1(nc.sbuf_tensor(f"a1tl{i}", [1, 256], F32)) for i in range(2)]
            c1 = {nm: E1(nc.sbuf_tensor("a1" + nm, [128, 4], F32))
                  for nm in "lnv rstd".split()}
            ps_s = E1(nc.psum_tensor("a1ps", [128, 1024], F32))
            ps_big = E1(nc.psum_tensor("a1pb", [128, 2048], F32))
            ps_tr1 = [E1(nc.psum_tensor(f"a1ptr{i}", [128, 1024], BF16)) for i in range(2)]

            nld_tab = {}
            nld = 0
            for g in range(NG):
                for h in range(H):
                    for li in range(len(_lspans(g))):
                        nld_tab[(g, h, li)] = nld
                        nld += 1
            TOT_L = nld

            with nc.Block("a1") as blk:
                @blk.sync
                def _(sy):
                    nl_c = [0]

                    def lstores(g):
                        for h in range(H):
                            for li, (a, b) in enumerate(_lspans(g)):
                                nl = nld_tab[(g, h, li)]
                                wg(sy, soV, ("a1lev", g, h, li))
                                slp[nl % 2].inc(sy.dma_start(
                                    l_tab2[h * NG + g:h * NG + g + 1, a:b],
                                    tmpl[nl % 2][0:1, 0:b - a]), 16)

                    for g in range(NG):
                        for i in range(4):
                            s_ = 4 * g + i
                            if s_ >= 4:
                                wg(sy, soV, ("a1xnb", s_ - 4))
                            sxsb[s_ % 4].inc(sy.dma_start(xs[s_ % 4][:], x_str[s_ * 128:(s_ + 1) * 128, :]), 16)
                        if g >= 1:
                            lstores(g - 1)
                    lstores(NG - 1)
                    sy.wait_ge(slp[0].h, slp[0].n)
                    sy.wait_ge(slp[1].h, slp[1].n)

                @blk.vector
                def _(v):
                    for i in range(4):
                        cv(v, "memset", vg[i][:].rearrange("p (h j) -> p h j", j=65)[:, :, 64:65], 1.0)

                    def stats(g):
                        for i in range(4):
                            s_ = 4 * g + i
                            v.wait_ge(sxsb[s_ % 4].h, 16 * (4 + s_ // 4 + 1))
                            cv(v, "bn_stats", bst1[:, i * 12:i * 12 + 6], xs[s_ % 4][:, 0:512])
                            cv(v, "bn_stats", bst1[:, i * 12 + 6:i * 12 + 12], xs[s_ % 4][:, 512:1024])
                        for i in range(4):
                            cv(v, "bn_aggr", mv1[:, i * 2:i * 2 + 2], bst1[:, i * 12:i * 12 + 12])
                        mark(("a1var", g), soV)

                    def xnbs(g):
                        wg(v, soA, ("a1rstd", g))
                        for i in range(4):
                            s_ = 4 * g + i
                            if s_ >= 2:
                                wg(v, soT, ("a1tr", (s_ - 2) // 4, (s_ - 2) % 4))
                            cv(v, "tensor_scalar", xnb1[s_ % 2][:], xs[s_ % 4][:], mv1[:, 2 * i:2 * i + 1],
                               c1["rstd"][:, i:i + 1], ALU.subtract, ALU.mult)
                            mark(("a1xnb", s_), soV)

                    stats(0)
                    xnbs(0)
                    for g in range(NG):
                        for i in range(4):
                            wg(v, soT, ("a1vm", g, i))
                            if g >= 1:
                                wg(v, soT, ("a1pv", g - 1, 15))
                            rb = 1024 * (i % 2)
                            cv(v, "tensor_copy",
                               vg[i][:].rearrange("p (h j) -> p h j", j=65)[:, :, 0:64],
                               ps_big[:, rb:rb + 1024].rearrange("p (h j) -> p h j", j=64))
                            mark(("a1ve", g, i), soV)
                        if g + 1 < NG:
                            stats(g + 1)
                        qlo = 128 * g
                        for h in range(H):
                            wg(v, soT, ("a1pv", g, h))
                            hc, r0 = h // 2, 64 * (h % 2)
                            dstap = OTacc[r0:r0 + 64, hc * TOWN + qlo:(hc + 1) * TOWN]
                            if g == 0:
                                cv(v, "tensor_copy", dstap, ps_big[0:64, qlo:TOWN])
                            else:
                                cv(v, "tensor_add", dstap, dstap, ps_big[0:64, qlo:TOWN])
                            for li, (a, b) in enumerate(_lspans(g)):
                                nl = nld_tab[(g, h, li)]
                                if nl >= 2:
                                    v.wait_ge(slp[nl % 2].h, 16 * (nl // 2))
                                cv(v, "tensor_copy", tmpl[nl % 2][0:1, 0:b - a], ps_big[64:65, a:b])
                                mark(("a1lev", g, h, li), soV)
                            mark(("a1ote", g, h), soV)
                        if g + 1 < NG:
                            xnbs(g + 1)

                @blk.tensor
                def _(t):
                    t.wait_ge(sw.h, 64)

                    def kv(j):
                        for i in range(4):
                            wg(t, soV, ("a1xnb", 4 * j + i))
                            for b_ in range(8):
                                ntr = 32 * j + 8 * i + b_
                                if ntr >= 2:
                                    wg(t, soA, ("a1te", ntr - 2))
                                soT.inc(t.transpose(ps_tr1[ntr % 2][:, 0:128],
                                                    xnb1[(4 * j + i) % 2][:, b_ * 128:(b_ + 1) * 128], id_b[:]))
                                mark(("a1trj", ntr), soT)
                            mark(("a1tr", j, i), soT)
                        wg(t, soA, ("a1te", 32 * j + 31))
                        if j >= 1:
                            wg(t, soV, ("a1ote", j - 1, 15))
                        for p8 in range(8):
                            rb = (p8 % 2) * 512
                            if p8 >= 2:
                                wg(t, soA, ("a1kt", j, p8 - 2))
                            ins = None
                            for cd in range(8):
                                ins = t.matmul(ps_big[:, rb:rb + 512],
                                               wq_kv[:, cd * 2048 + p8 * 128:cd * 2048 + (p8 + 1) * 128],
                                               xnt1[:, cd * 512:(cd + 1) * 512],
                                               start=(cd == 0), stop=(cd == 7))
                            soT.inc(ins)
                            mark(("a1ktm", j, p8), soT)
                        wg(t, soA, ("a1kt", j, 7))
                        for i in range(4):
                            rb = 1024 * (i % 2)
                            if i >= 2:
                                wg(t, soV, ("a1ve", j, i - 2))
                            ins = None
                            for cd in range(8):
                                lhsT = xnt1[:, cd * 512 + i * 128:cd * 512 + i * 128 + 128]
                                ins = t.matmul(ps_big[:, rb:rb + 512], lhsT,
                                               wq_kv[:, cd * 2048 + 1024:cd * 2048 + 1536],
                                               start=(cd == 0), stop=(cd == 7))
                                ins = t.matmul(ps_big[:, rb + 512:rb + 1024], lhsT,
                                               wq_kv[:, cd * 2048 + 1536:cd * 2048 + 2048],
                                               start=(cd == 0), stop=(cd == 7))
                            soT.inc(ins)
                            mark(("a1vm", j, i), soT)

                    nsc = [0]

                    def scores(g, h):
                        gp, r0 = h // 2, 64 * (h % 2)
                        for k in range(4):
                            for pi, (a, b) in enumerate(_pieces(g)):
                                jj = nsc[0]
                                nsc[0] += 1
                                if jj >= 2:
                                    wg(t, soA, ("a1expj", jj - 2))
                                ins = t.matmul(ps_s[:, (jj % 2) * 512:(jj % 2) * 512 + (b - a)],
                                               ktg[r0:r0 + 64, gp * 512 + k * 128:gp * 512 + (k + 1) * 128],
                                               QT[gp][r0:r0 + 64, a:b], start=True, stop=(pi != 0))
                                if pi == 0:
                                    ins = t.matmul(ps_s[:, (jj % 2) * 512:(jj % 2) * 512 + 128],
                                                   maskb[:, k * 128:(k + 1) * 128], id_b[:],
                                                   start=False, stop=True)
                                soT.inc(ins)
                                mark(("a1smm", jj), soT)

                    def pv(g, h):
                        wg(t, soA, ("a1exp", g, h))
                        if h == 0:
                            wg(t, soV, ("a1ve", g, 3))
                        if h >= 1:
                            wg(t, soV, ("a1ote", g, h - 1))
                        ins = None
                        for (a, b) in _pieces(g):
                            for k in range(4):
                                ins = t.matmul(ps_big[0:65, a:b], vg[k][:, h * 65:(h + 1) * 65],
                                               PT[h % 2][k][:, a:b], start=(k == 0), stop=(k == 3))
                        soT.inc(ins)
                        mark(("a1pv", g, h), soT)

                    kv(0)
                    for g in range(NG):
                        scores(g, 0)
                        for h in range(H):
                            if h + 1 < H:
                                scores(g, h + 1)
                            pv(g, h)
                        if g + 1 < NG:
                            kv(g + 1)

                @blk.scalar
                def _(s):
                    def rstd_chain(g):
                        wg(s, soV, ("a1var", g))
                        for i in range(4):
                            cs(s, "activation", c1["lnv"][:, i:i + 1], mv1[:, 2 * i + 1:2 * i + 2], AF.Ln, bias=eps5[:])
                        for i in range(4):
                            cs(s, "activation", c1["rstd"][:, i:i + 1], c1["lnv"][:, i:i + 1], AF.Exp, scale=-0.5)
                        mark(("a1rstd", g), soA)

                    def kv_evac(j):
                        for i in range(4):
                            for b_ in range(8):
                                ntr = 32 * j + 8 * i + b_
                                wg(s, soT, ("a1trj", ntr))
                                if j >= 1 and i == 0 and b_ == 0:
                                    wg(s, soT, ("a1vm", j - 1, 3))
                                cs(s, "activation",
                                   xnt1[:, b_ * 512 + i * 128:b_ * 512 + i * 128 + 128],
                                   ps_tr1[ntr % 2][:, 0:128], AF.Copy)
                                mark(("a1te", ntr), soA)
                        for p8 in range(8):
                            rb = (p8 % 2) * 512
                            wg(s, soT, ("a1ktm", j, p8))
                            if j >= 1 and p8 == 0:
                                wg(s, soT, ("a1smm", scg[j - 1] - 1))
                            cs(s, "activation", ktg[:, p8 * 512:(p8 + 1) * 512],
                               ps_big[:, rb:rb + 512], AF.Copy)
                            mark(("a1kt", j, p8), soA)

                    rstd_chain(0)
                    kv_evac(0)
                    nsc = 0
                    for g in range(NG):
                        if g + 1 < NG:
                            rstd_chain(g + 1)
                        for h in range(H):
                            if h >= 2:
                                wg(s, soT, ("a1pv", g, h - 2))
                            elif g >= 1:
                                wg(s, soT, ("a1pv", g - 1, 14 + h))
                            for k in range(4):
                                for (a, b) in _pieces(g):
                                    wg(s, soT, ("a1smm", nsc))
                                    cs(s, "activation", PT[h % 2][k][:, a:b],
                                       ps_s[:, (nsc % 2) * 512:(nsc % 2) * 512 + (b - a)],
                                       AF.Exp, scale=SC)
                                    mark(("a1expj", nsc), soA)
                                    nsc += 1
                            mark(("a1exp", g, h), soA)
                        if g + 1 < NG:
                            kv_evac(g + 1)

        ctxB.close()
        with ExitStack() as woc:
            Ew = woc.enter_context
            OTb = Ew(nc.sbuf_tensor("OTb", [128, 8 * TOWN], BF16))
            on16 = Ew(nc.sbuf_tensor("on16", [16, 128], F32))
            wos = Ew(nc.sbuf_tensor("wos", [128, 8 * D], BF16))
            lts = [Ew(nc.sbuf_tensor(f"lts{i}", [16, TOWN], F32)) for i in range(2)]
            rlb = Ew(nc.sbuf_tensor("rlb", [128, TOWN], F32))
            xo = [Ew(nc.sbuf_tensor(f"xo{i}", [128, D], F32)) for i in range(2)]
            axt = [Ew(nc.sbuf_tensor(f"axt{i}", [128, D], F32)) for i in range(2)]
            ps_lb = Ew(nc.psum_tensor("ps_lb", [128, TOWN], F32))
            ps_wo = [Ew(nc.psum_tensor(f"ps_wo{i}", [128, D], F32)) for i in range(2)]

            A0SP = ssp.n

            with nc.Block("wo") as blk:
                @blk.gpsimd
                def _(g_):
                    sw.inc(g_.dma_start(wos[:].rearrange("p (c j) -> p c j", c=8),
                                        w_o_in.rearrange("(c p) j -> p c j", p=128)), 16)

                @blk.sync
                def _(sy):
                    for h in range(H):
                        if h >= 2:
                            wg(sy, soT, ("wolb", h - 2))
                        slt[h % 2].inc(sy.dma_start(lts[h % 2][:], l_tab2[h * NG:(h + 1) * NG, :]), 16)
                    for tc in range(NCH):
                        if tc >= 2:
                            wg(sy, soV, ("woax", tc - 2))
                        sxo[tc % 2].inc(sy.dma_start(xo[tc % 2][:], x_own[tc * 128:(tc + 1) * 128, :]), 16)
                        if tc >= 2:
                            wg(sy, soV, ("woax", tc - 2))
                            sat[tc % 2].inc(sy.dma_start(attnx_sp[(tc - 2) * 128:(tc - 1) * 128, :],
                                                         axt[tc % 2][:]), 16)
                    for tc in range(NCH - 2, NCH):
                        wg(sy, soV, ("woax", tc))
                        sat[tc % 2].inc(sy.dma_start(attnx_sp[tc * 128:(tc + 1) * 128, :], axt[tc % 2][:]), 16)
                    if dbg:
                        wg(sy, soV, ("wootb", H - 1))
                        ssp.inc(sy.dma_start(d_ot, OTacc[:]), 16)
                    sy.wait_ge(ssp.h, ssp.n)
                    sy.wait_ge(sat[0].h, sat[0].n)
                    sy.wait_ge(sat[1].h, sat[1].n)

                @blk.tensor
                def _(t):
                    t.wait_ge(sw.h, 80)
                    wg(t, soV, "on16")
                    for h in range(H):
                        t.wait_ge(slt[h % 2].h, 16 * (h // 2 + 1))
                        if h >= 1:
                            wg(t, soV, ("worl", h - 1))
                        ins = None
                        for pc in range(4):
                            ins = t.matmul(ps_lb[:, pc * 512:(pc + 1) * 512], on16[:],
                                           lts[h % 2][:, pc * 512:(pc + 1) * 512], start=True, stop=True)
                        soT.inc(ins)
                        mark(("wolb", h), soT)
                    for tc in range(NCH):
                        wg(t, soV, ("wootb", H - 1))
                        if tc >= 2:
                            wg(t, soV, ("woax", tc - 2))
                        ins = None
                        for cd in range(8):
                            lhsT = OTb[:, cd * TOWN + tc * 128:cd * TOWN + (tc + 1) * 128]
                            ins = t.matmul(ps_wo[tc % 2][:, 0:512], lhsT, wos[:, cd * D:cd * D + 512],
                                           start=(cd == 0), stop=(cd == 7))
                            ins = t.matmul(ps_wo[tc % 2][:, 512:1024], lhsT, wos[:, cd * D + 512:(cd + 1) * D],
                                           start=(cd == 0), stop=(cd == 7))
                        soT.inc(ins)
                        mark(("womm", tc), soT)

                @blk.vector
                def _(v):
                    cv(v, "memset", on16[:], 1.0)
                    mark("on16", soV)
                    for h in range(H):
                        wg(v, soT, ("wolb", h))
                        cv(v, "reciprocal", rlb[:], ps_lb[:])
                        mark(("worl", h), soV)
                        hc, r0 = h // 2, 64 * (h % 2)
                        cv(v, "tensor_mul", OTb[r0:r0 + 64, hc * TOWN:(hc + 1) * TOWN],
                           OTacc[r0:r0 + 64, hc * TOWN:(hc + 1) * TOWN], rlb[r0:r0 + 64, :])
                        mark(("wootb", h), soV)
                    for tc in range(NCH):
                        wg(v, soT, ("womm", tc))
                        v.wait_ge(sxo[tc % 2].h, 16 * (tc // 2 + 1))
                        if tc >= 2:
                            v.wait_ge(sat[tc % 2].h, 16 * ((tc - 2) // 2 + 1))
                        cv(v, "tensor_add", axt[tc % 2][:], xo[tc % 2][:], ps_wo[tc % 2][:])
                        mark(("woax", tc), soV)

    A1SP = ssp.n
    with ExitStack() as mctx:
        Em = mctx.enter_context
        y_acc = Em(nc.sbuf_tensor("y_acc", [128, NCH * D], F32))
        X2 = Em(nc.sbuf_tensor("mX2", [128, 8 * TOWN], BF16))
        hsl = Em(nc.sbuf_tensor("mh", [128, 8 * TOWN], BF16))
        w2s = [Em(nc.sbuf_tensor(f"mw2{i}", [128, 8 * D], BF16)) for i in range(2)]
        w1t = [Em(nc.sbuf_tensor(f"mw1{i}", [128, 8 * 128], BF16)) for i in range(2)]
        b1s = Em(nc.sbuf_tensor("mb1", [128, NE * 32], F32))
        b2s = Em(nc.sbuf_tensor("mb2", [8, D], F32))
        coefT = Em(nc.sbuf_tensor("mcoefT", [8, NCH * 128], F32))
        coef = Em(nc.sbuf_tensor("mcoef", [128, NCH * NE], F32))
        axt2 = [Em(nc.sbuf_tensor(f"max{i}", [128, D], F32)) for i in range(2)]
        outt = [Em(nc.sbuf_tensor(f"mo{i}", [128, D], F32)) for i in range(2)]
        ps_h = [Em(nc.psum_tensor(f"mph{i}", [128, 1024], F32)) for i in range(2)]
        ps_y = [Em(nc.psum_tensor(f"mpy{i}", [128, 1024], F32)) for i in range(2)]

        with nc.Block("moe") as blk:
            @blk.sync
            def _(sy):
                sy.wait_ge(ssp.h, A1SP)
                sy.wait_ge(sat[0].h, sat[0].n)
                sy.wait_ge(sat[1].h, sat[1].n)
                smc.inc(sy.dma_start(X2[:], xn2t_sp), 16)
                smc.inc(sy.dma_start(b2s[:], b2_in), 16)
                smc.inc(sy.dma_start(b1s[:], b1_t), 16)
                smc.inc(sy.dma_start(coefT[:], coeft_sp), 16)
                smc.inc(sy.dma_start(coef[:], coef_sp), 16)
                for tc in range(NCH):
                    if tc >= 2:
                        wg(sy, soV, ("mfin", tc - 2))
                    sax[tc % 2].inc(sy.dma_start(axt2[tc % 2][:], attnx_sp[tc * 128:(tc + 1) * 128, :]), 16)
                    if tc >= 2:
                        wg(sy, soV, ("mfin", tc - 2))
                        sot[tc % 2].inc(sy.dma_start(out_t[(tc - 2) * 128:(tc - 1) * 128, :], outt[tc % 2][:]), 16)
                for tc in range(NCH - 2, NCH):
                    wg(sy, soV, ("mfin", tc))
                    sot[tc % 2].inc(sy.dma_start(out_t[tc * 128:(tc + 1) * 128, :], outt[tc % 2][:]), 16)
                sy.wait_ge(sot[0].h, 16 * (NCH // 2))
                sy.wait_ge(sot[1].h, 16 * (NCH // 2))

            @blk.gpsimd
            def _(g_):
                ti = 0
                for e in range(NE):
                    for fq in range(4):
                        q = e * 4 + fq
                        if q >= 2:
                            wg(g_, soT, ("mym", q - 2, NCH - 1))
                        r0 = e * FF + fq * 1024
                        sw2[q % 2].inc(g_.dma_start(w2s[q % 2][:].rearrange("p (c j) -> p c j", c=8),
                                                    w2_2d[r0:r0 + 1024, :].rearrange("(c p) j -> p c j", p=128)), 16)
                        for fc in range(8):
                            if ti >= 2:
                                wg(g_, soT, ("mhm", ti - 2, 1))
                            cl = fq * 1024 + fc * 128
                            sw1[ti % 2].inc(g_.dma_start(
                                w1t[ti % 2][:].rearrange("p (c f) -> p c f", c=8),
                                w1_2d[e * D:(e + 1) * D, cl:cl + 128].rearrange("(c p) f -> p c f", p=128)), 16)
                            ti += 1

            @blk.tensor
            def _(t):
                t.wait_ge(smc.h, 80)
                for tc in range(NCH):
                    if tc >= 2:
                        wg(t, soV, ("mye", -1, tc - 2))
                    ins = t.matmul(ps_y[tc % 2][:, 0:512], coefT[:, tc * 128:(tc + 1) * 128],
                                   b2s[:, 0:512], start=True, stop=True)
                    ins = t.matmul(ps_y[tc % 2][:, 512:1024], coefT[:, tc * 128:(tc + 1) * 128],
                                   b2s[:, 512:1024], start=True, stop=True)
                    soT.inc(ins)
                    mark(("myi", tc), soT)
                ti = 0
                for e in range(NE):
                    for fq in range(4):
                        q = e * 4 + fq
                        for fc in range(8):
                            t.wait_ge(sw1[ti % 2].h, 16 * (ti // 2 + 1))
                            for sub in range(2):
                                if ti > 0:
                                    wg(t, soA, ("msi", ti - 1, sub))
                                ins = None
                                for cd in range(8):
                                    co = cd * TOWN + sub * 1024
                                    ins = t.matmul(ps_h[sub][:, 0:512], w1t[ti % 2][:, cd * 128:(cd + 1) * 128],
                                                   X2[:, co:co + 512], start=(cd == 0), stop=(cd == 7))
                                    ins = t.matmul(ps_h[sub][:, 512:1024], w1t[ti % 2][:, cd * 128:(cd + 1) * 128],
                                                   X2[:, co + 512:co + 1024], start=(cd == 0), stop=(cd == 7))
                                soT.inc(ins)
                                mark(("mhm", ti, sub), soT)
                            ti += 1
                        wg(t, soA, ("msi", q * 8 + 7, 1))
                        t.wait_ge(sw2[q % 2].h, 16 * (q // 2 + 1))
                        for tc in range(NCH):
                            if tc >= 2:
                                wg(t, soV, ("mye", q, tc - 2))
                            elif q >= 1:
                                wg(t, soV, ("mye", q - 1, 14 + tc))
                            else:
                                wg(t, soV, ("mye", -1, 14 + tc % 2))
                            ins = None
                            for fc in range(8):
                                lhsT = hsl[:, fc * TOWN + tc * 128:fc * TOWN + (tc + 1) * 128]
                                ins = t.matmul(ps_y[tc % 2][:, 0:512], lhsT, w2s[q % 2][:, fc * D:fc * D + 512],
                                               start=(fc == 0), stop=(fc == 7))
                                ins = t.matmul(ps_y[tc % 2][:, 512:1024], lhsT,
                                               w2s[q % 2][:, fc * D + 512:(fc + 1) * D],
                                               start=(fc == 0), stop=(fc == 7))
                            soT.inc(ins)
                            mark(("mym", q, tc), soT)

            @blk.scalar
            def _(s):
                ti = 0
                for e in range(NE):
                    for fq in range(4):
                        q = e * 4 + fq
                        for fc in range(8):
                            if q >= 1 and fc == 0:
                                wg(s, soT, ("mym", q - 1, NCH - 1))
                            for sub in range(2):
                                wg(s, soT, ("mhm", ti, sub))
                                cs(s, "activation", hsl[:, fc * TOWN + sub * 1024:fc * TOWN + sub * 1024 + 1024],
                                   ps_h[sub][:], AF.Silu,
                                   bias=b1s[:, e * 32 + fq * 8 + fc:e * 32 + fq * 8 + fc + 1])
                                mark(("msi", ti, sub), soA)
                            ti += 1

            @blk.vector
            def _(v):
                for tc in range(NCH):
                    wg(v, soT, ("myi", tc))
                    cv(v, "tensor_copy", y_acc[:, tc * D:(tc + 1) * D], ps_y[tc % 2][:])
                    mark(("mye", -1, tc), soV)
                for e in range(NE):
                    for fq in range(4):
                        q = e * 4 + fq
                        for tc in range(NCH):
                            wg(v, soT, ("mym", q, tc))
                            cv(v, "scalar_tensor_tensor",
                               y_acc[:, tc * D:(tc + 1) * D], ps_y[tc % 2][:],
                               coef[:, tc * NE + e:tc * NE + e + 1],
                               y_acc[:, tc * D:(tc + 1) * D], ALU.mult, ALU.add)
                            mark(("mye", q, tc), soV)
                for tc in range(NCH):
                    v.wait_ge(sax[tc % 2].h, 16 * (tc // 2 + 1))
                    if tc >= 2:
                        v.wait_ge(sot[tc % 2].h, 16 * ((tc - 2) // 2 + 1))
                    cv(v, "tensor_add", outt[tc % 2][:], axt2[tc % 2][:], y_acc[:, tc * D:(tc + 1) * D])
                    mark(("mfin", tc), soV)

    ctx.close()
    return nc


def build_nc(dbg=False, upto="full", sub=""):
    m1 = {}
    _build(dbg, upto, {}, m1, sub)
    m2 = {}
    nc = _build(dbg, upto, m1, m2, sub)
    assert m1 == m2, "nondeterministic build marks"
    return nc


_CACHE = {}


def _host_prep(inputs):
    x = np.asarray(inputs["L0_attn_x"], dtype=np.float32)
    wqkv = np.asarray(inputs["L0_attn_w_qkv"], dtype=np.float32)
    idx = np.arange(3 * D).reshape(H, 3, 64)
    perm = np.concatenate([idx[:, 0].ravel(), idx[:, 1].ravel(), idx[:, 2].ravel()])
    wqkv_r = np.ascontiguousarray(wqkv[:, perm])
    w_o = np.ascontiguousarray(np.asarray(inputs["L0_attn_w_o"], dtype=np.float32))
    rms_bc = np.ascontiguousarray(np.broadcast_to(
        np.asarray(inputs["L0_ffn_rms_w"], dtype=np.float32), (128, D)))
    gate_t = np.ascontiguousarray(np.asarray(inputs["L0_ffn_gate_w"], dtype=np.float32).T)
    b1 = np.asarray(inputs["L0_ffn_b1"], dtype=np.float32)
    b1_t = np.ascontiguousarray(b1.reshape(NE, 32, 128).transpose(2, 0, 1).reshape(128, NE * 32))
    b2 = np.ascontiguousarray(np.asarray(inputs["L0_ffn_b2"], dtype=np.float32))
    w1 = np.ascontiguousarray(np.asarray(inputs["L0_ffn_W1"], dtype=np.float32).reshape(NE * D, FF))
    w2 = np.ascontiguousarray(np.asarray(inputs["L0_ffn_W2"], dtype=np.float32).reshape(NE * FF, D))
    ident = np.eye(128, dtype=np.float32)
    tri = np.where(np.triu(np.ones((128, 128), np.float32)) > 0, 0.0, -4096.0).astype(np.float32)
    in_maps = []
    rows_all = []
    for c in range(8):
        b_, cc = c // 4, c % 4
        rows = np.concatenate([np.arange((4 * g + cc) * 128, (4 * g + cc) * 128 + 128)
                               for g in range(NG)])
        rows_all.append((b_, rows))
        mk = np.zeros((4, 128, 128), dtype=np.float32)
        for r in range(4):
            if r < cc:
                mk[r] = 0.0
            elif r == cc:
                mk[r] = tri.T
            else:
                mk[r] = -4096.0
        in_maps.append(dict(
            x_own=np.ascontiguousarray(x[b_].reshape(S, D)[rows]),
            x_str=np.ascontiguousarray(x[b_]),
            wqkv=wqkv_r, w_o_in=w_o, rms_bc=rms_bc, gate_t=gate_t, b1_t=b1_t,
            b2_in=b2, w1_2d=w1, w2_2d=w2, masks=np.ascontiguousarray(mk), ident=ident))
    return in_maps, rows_all


def kernel(**inputs):
    if "nc" not in _CACHE:
        _CACHE["nc"] = build_nc(dbg=False, upto="full")
    nc = _CACHE["nc"]
    in_maps, rows_all = _host_prep(inputs)
    res = run_bass_kernel_spmd(nc, in_maps, core_ids=list(range(8)))
    out = np.empty((B, S, D), dtype=np.float32)
    for c in range(8):
        b_, rows = rows_all[c]
        out[b_].reshape(S, D)[rows] = res.results[c]["out"]
    return out
